# version 01
# baseline (speedup 1.0000x reference)
import numpy as np
import ml_dtypes
from contextlib import ExitStack

import concourse.bass as bass
import concourse.mybir as mybir
from concourse.bass_utils import run_bass_kernel_spmd

F32 = mybir.dt.float32
BF16 = mybir.dt.bfloat16
I32 = mybir.dt.int32
AF = mybir.ActivationFunctionType
ALU = mybir.AluOpType
AX = mybir.AxisListType
ds = bass.ds

B, T, C = 4, 4096, 2048
H, D = 8, 128
QD, KD = 2048, 1024
INTER = 5632
NI = INTER // 128
GATE_OFFSET = 6.906768
OWN = 2048
NBLK = T // 128
VBLK = NBLK + 4
C_BIAS = -0.5 * float(np.log(128.0))
EPS_BIAS = float(np.log(1e-6))

CONSTS = {
    "c_bias": C_BIAS, "eps_bias": EPS_BIAS, "ln_eps": 1e-5, "one": 1.0,
    "go": GATE_OFFSET, "zero": 0.0, "inv_c": 1.0 / 2048.0, "neg1": -1.0,
}
CONST_COLS = list(CONSTS.keys())

J_KOFF, J_VOFF, J_CV, J_EPS = 0, 1, 2, 3
P1_SUB = 2
P1_GRAN = 6


class TL:
    def __init__(self, name):
        self.name, self.n, self.sem = name, 0, None


class Emitter:
    ENGS = ("tensor", "vector", "scalar", "gpsimd", "sync")

    def __init__(self):
        self.ops = {k: [] for k in self.ENGS}

    def wait(self, eng, tl, n):
        if n and n > 0:
            self.ops[eng].append(lambda e, s=tl, v=n: e.wait_ge(s.sem, v))

    def do(self, eng, f, inc=None, inc_k=1):
        if inc is None:
            self.ops[eng].append(f)
            return None
        inc.n += inc_k
        self.ops[eng].append(lambda e, f=f, s=inc, k=inc_k: f(e).then_inc(s.sem, k))
        return inc.n

    def replay(self, nc):
        with nc.Block() as blk:
            for name in self.ENGS:
                oplist = self.ops[name]
                if not oplist:
                    continue

                def body(e, oplist=oplist):
                    for f in oplist:
                        f(e)
                getattr(blk, name)(body)


_REGC = [0]
def _rn():
    _REGC[0] += 1
    return f"dynr{_REGC[0]}"


def build_program(debug=(), max_phase=5, detect_races=True):
    nc = bass.Bass("TRN2", target_bir_lowering=False,
                   detect_race_conditions=detect_races)
    dbg = set(debug)

    def scratch(name, shape, dt):
        return nc.dram_tensor(name, shape, dt,
                              kind="ExternalOutput" if name in dbg else "Internal")

    x_all = nc.dram_tensor("x_all", [T, C], F32, kind="ExternalInput")
    x_own = nc.dram_tensor("x_own", [OWN, C], F32, kind="ExternalInput")
    wq = nc.dram_tensor("wq", [C, QD], F32, kind="ExternalInput")
    wk = nc.dram_tensor("wk", [C, KD], F32, kind="ExternalInput")
    wv = nc.dram_tensor("wv", [C, KD], F32, kind="ExternalInput")
    wg = nc.dram_tensor("wg", [C, H], F32, kind="ExternalInput")
    wc = nc.dram_tensor("wc", [QD, C], F32, kind="ExternalInput")
    bc = nc.dram_tensor("bc", [1, C], F32, kind="ExternalInput")
    gate_wt = nc.dram_tensor("gate_wt", [C, INTER], F32, kind="ExternalInput")
    up_wt = nc.dram_tensor("up_wt", [C, INTER], F32, kind="ExternalInput")
    down_wt = nc.dram_tensor("down_wt", [INTER, C], F32, kind="ExternalInput")
    gate_st = nc.dram_tensor("gate_st", [128, NI], F32, kind="ExternalInput")
    up_st = nc.dram_tensor("up_st", [128, NI], F32, kind="ExternalInput")
    down_s = nc.dram_tensor("down_s", [1, C], F32, kind="ExternalInput")
    tri = nc.dram_tensor("tri", [128, 128], BF16, kind="ExternalInput")
    ublk = nc.dram_tensor("ublk", [128, 128], F32, kind="ExternalInput")
    ones32 = nc.dram_tensor("ones32", [32, 128], F32, kind="ExternalInput")
    ident = nc.dram_tensor("ident", [128, 128], F32, kind="ExternalInput")
    consts = nc.dram_tensor("consts", [128, len(CONST_COLS)], F32, kind="ExternalInput")
    dyn = nc.dram_tensor("dyn", [1, 96], I32, kind="ExternalInput")

    out_own = nc.dram_tensor("out_own", [OWN, C], F32, kind="ExternalOutput")

    xn_scr = scratch("xn_scr", [T, C], BF16)
    xn_own_scr = scratch("xn_own_scr", [OWN, C], BF16)
    q_scr = scratch("q_scr", [QD, OWN], BF16)
    k_scr = scratch("k_scr", [KD, T + 1024], BF16)
    v_scr = scratch("v_scr", [T + 1024, 1032], BF16)
    y_scr = scratch("y_scr", [OWN, QD], BF16)
    gsig_scr = scratch("gsig_scr", [OWN, C], F32)
    tmp_scr = scratch("tmp_scr", [32, 8], F32)
    logg_scr = scratch("logg_scr", [128, 256], F32)
    cg_scr = scratch("cg_scr", [128, 320], F32)
    eps_scr = scratch("eps_scr", [128, 256], F32)
    hdn_scr = scratch("hdn_scr", [NI, 128, 1024], BF16)
    sg_scr = scratch("sg_scr", [4, 128, 1024], BF16)
    dbg_scr = scratch("dbg_scr", [128, 1024], F32)
    ab_scr = scratch("ab_scr", [8, 128, 512], BF16)
    py_scr = scratch("py_scr", [128, 258], F32)
    qt_scr = scratch("qt_scr", [128, 512], BF16)

    with ExitStack() as outer:
        oe = outer.enter_context

        pe, act, dve, pool = TL("pe"), TL("act"), TL("dve"), TL("pool")
        qs, qg = TL("qs"), TL("qg")
        alltl = [pe, act, dve, pool, qs, qg]
        for tl in alltl:
            tl.sem = oe(nc.semaphore(tl.name))

        consts_sb = oe(nc.sbuf_tensor("consts_sb", [128, len(CONST_COLS)], F32))
        dyn_sb = oe(nc.sbuf_tensor("dyn_sb", [1, 96], I32))
        tri_sb = oe(nc.sbuf_tensor("tri_sb", [128, 128], BF16))
        ublk_sb = oe(nc.sbuf_tensor("ublk_sb", [128, 128], F32))
        ones32_sb = oe(nc.sbuf_tensor("ones32_sb", [32, 128], F32))
        ident_sb = oe(nc.sbuf_tensor("ident_sb", [128, 128], F32))
        ones1f = oe(nc.sbuf_tensor("ones1f", [1, 128], F32))
        gpt = oe(nc.sbuf_tensor("gpt", [128, 256], F32))
        logg_pt = oe(nc.sbuf_tensor("logg_pt", [128, 256], F32))
        c_virt = oe(nc.sbuf_tensor("c_virt", [128, VBLK * 8], F32))
        eps_own = oe(nc.sbuf_tensor("eps_own", [128, 16 * 8], F32))
        bsum = oe(nc.sbuf_tensor("bsum", [32, 8], F32))

        def cc(name):
            i = CONST_COLS.index(name)
            return consts_sb[:, i:i + 1]

        for k, v in CONSTS.items():
            nc.const_aps.register(F32, float(v), cc(k))

        def dyn_gather(dst_ap, src_dram, j, width, max_val, rows=False):
            def f(e):
                with e.register(_rn()) as r:
                    e.reg_load(r, dyn_sb[0:1, j:j + 1])
                    off = e.snap(r, donate=True, min_val=0, max_val=max_val)
                    if rows:
                        s = src_dram[ds(off, width), :]
                    else:
                        s = src_dram[:, ds(off, width)]
                    return e.dma_start(dst_ap, s)
            return f

        def barrier(emm):
            snap = [(tl, tl.n) for tl in alltl]
            for eng in Emitter.ENGS:
                for tl, n in snap:
                    emm.wait(eng, tl, n)

        em = Emitter()
        for sb, dr in ((consts_sb, consts), (dyn_sb, dyn), (tri_sb, tri),
                       (ublk_sb, ublk), (ones32_sb, ones32), (ident_sb, ident)):
            em.do("sync", lambda e, sb=sb, dr=dr: e.dma_start(sb[:], dr[:, :]),
                  inc=qs, inc_k=16)
        em.do("vector", lambda e: e.memset(ones1f[:], 1.0), inc=dve)
        em.replay(nc)

        if max_phase < 1:
            return nc
        with ExitStack() as ph:
            pp = ph.enter_context
            em = Emitter()
            barrier(em)
            NR = 3
            x_b = [pp(nc.sbuf_tensor(f"a_x{i}", [128, C], F32)) for i in range(NR)]
            xq_b = [pp(nc.sbuf_tensor(f"a_xq{i}", [128, C], F32)) for i in range(2)]
            xn_b = [pp(nc.sbuf_tensor(f"a_xn{i}", [128, C], BF16)) for i in range(NR)]
            st_b = [pp(nc.sbuf_tensor(f"a_st{i}", [128, 8], F32)) for i in range(NR)]

            tiles = [(x_all, xn_scr, i) for i in range(32)] + \
                    [(x_own, xn_own_scr, i) for i in range(16)]
            x_free = [0] * NR
            x_free_a = [0] * NR
            st_free = [0] * NR
            xq_free = [0] * 2
            xn_free = [0] * NR
            for ti, (src, dst, i) in enumerate(tiles):
                bi, qi = ti % NR, ti % 2
                xb, xnb, stb, xqb = x_b[bi], xn_b[bi], st_b[bi], xq_b[qi]
                em.wait("sync", dve, x_free[bi])
                em.wait("sync", act, x_free_a[bi])
                ld = em.do("sync", lambda e, xb=xb, src=src, i=i: e.dma_start(
                    xb[:], src[i * 128:(i + 1) * 128, :]), inc=qs, inc_k=16)
                em.wait("vector", qs, ld)
                em.wait("vector", act, st_free[bi])
                em.do("vector", lambda e, xb=xb, stb=stb: e.reduce_sum(
                    stb[:, 0:1], xb[:], axis=AX.X), inc=dve)
                em.wait("scalar", qs, ld)
                em.wait("scalar", act, xq_free[qi])
                sq = em.do("scalar", lambda e, xb=xb, xqb=xqb: e.activation(
                    xqb[:], xb[:], AF.Square), inc=act)
                x_free_a[bi] = act.n
                em.wait("vector", act, sq)
                s2n = em.do("vector", lambda e, xqb=xqb, stb=stb: e.reduce_sum(
                    stb[:, 1:2], xqb[:], axis=AX.X), inc=dve)
                em.wait("scalar", dve, s2n)
                ng = em.do("scalar", lambda e, stb=stb: e.mul(
                    stb[:, 2:3], stb[:, 0:1], -1.0 / 2048.0), inc=act)
                em.wait("vector", act, ng)
                vv = em.do("vector", lambda e, stb=stb: e.scalar_tensor_tensor(
                    stb[:, 4:5], stb[:, 0:1], stb[:, 2:3], stb[:, 1:2],
                    ALU.mult, ALU.add), inc=dve)
                em.wait("scalar", dve, vv)
                sd = em.do("scalar", lambda e, stb=stb: e.activation(
                    stb[:, 5:6], stb[:, 4:5], AF.Sqrt, scale=1.0 / 2048.0,
                    bias=cc("ln_eps")), inc=act)
                em.wait("vector", act, sd)
                rr = em.do("vector", lambda e, stb=stb: e.reciprocal(
                    stb[:, 6:7], stb[:, 5:6]), inc=dve)
                xc = em.do("vector", lambda e, xb=xb, xqb=xqb, stb=stb:
                    e.tensor_scalar(xqb[:], xb[:], stb[:, 2:3], None, ALU.add),
                    inc=dve)
                x_free[bi] = dve.n
                em.wait("scalar", dve, xc)
                em.wait("scalar", qs, xn_free[bi])
                xn = em.do("scalar", lambda e, xqb=xqb, xnb=xnb, stb=stb:
                    e.activation(xnb[:], xqb[:], AF.Copy, scale=stb[:, 6:7]),
                    inc=act)
                xq_free[qi] = act.n
                st_free[bi] = act.n
                em.wait("sync", act, xn)
                xn_free[bi] = em.do("sync", lambda e, xnb=xnb, dst=dst, i=i:
                    e.dma_start(dst[i * 128:(i + 1) * 128, :], xnb[:]),
                    inc=qs, inc_k=16)
            em.replay(nc)

        if max_phase < 2:
            return nc
        with ExitStack() as ph:
            pp = ph.enter_context
            em = Emitter()
            barrier(em)
            wq_sb = [pp(nc.sbuf_tensor(f"wq{k}", [128, QD], BF16)) for k in range(16)]
            wk_sb = [pp(nc.sbuf_tensor(f"wk{k}", [128, KD], BF16)) for k in range(16)]
            wv_sb = [pp(nc.sbuf_tensor(f"wv{k}", [128, KD], BF16)) for k in range(16)]
            wg_sb = [pp(nc.sbuf_tensor(f"wg{k}", [128, H], BF16)) for k in range(16)]
            for k in range(16):
                for sb, dr in ((wq_sb, wq), (wk_sb, wk), (wv_sb, wv), (wg_sb, wg)):
                    em.do("gpsimd", lambda e, sb=sb, dr=dr, k=k: e.dma_start(
                        sb[k][:], dr[k * 128:(k + 1) * 128, :]), inc=qg, inc_k=16)
            w_ld = qg.n

            xnf = [[pp(nc.sbuf_tensor(f"xnf{s}_{k}", [128, 512], BF16))
                    for k in range(16)] for s in range(2)]
            kv_st = [pp(nc.sbuf_tensor(f"kvst{i}", [128, 512], BF16)) for i in range(4)]
            v_st = [pp(nc.sbuf_tensor(f"vst{i}", [128, 1032], BF16)) for i in range(4)]
            g_st = [pp(nc.sbuf_tensor(f"gst{i}", [8, 512], F32)) for i in range(2)]
            zt = pp(nc.sbuf_tensor("zt", [128, 256], F32))
            azt = pp(nc.sbuf_tensor("azt", [128, 256], F32))
            zt2 = pp(nc.sbuf_tensor("zt2", [128, 256], F32))

            ps_a = [pp(nc.psum_tensor(f"ps_a{i}", [128, 512], F32)) for i in range(2)]
            ps_v = [pp(nc.psum_tensor(f"ps_v{i}", [128, 1024], F32)) for i in range(2)]
            ps_g = pp(nc.psum_tensor("ps_g", [8, 512], F32))
            ps_t = pp(nc.psum_tensor("ps_t", [128, 8], F32))

            em.do("vector", lambda e: e.memset(kv_st[0][:], 0.0), inc=dve)
            em.do("vector", lambda e: e.memset(v_st[0][:], 0.0), inc=dve)
            zf = dve.n
            em.wait("sync", dve, zf)
            for h in range(8):
                em.do("sync", lambda e, h=h: e.dma_start(
                    k_scr[h * 128:(h + 1) * 128, 0:512], kv_st[0][:]),
                    inc=qs, inc_k=16)
            for r4 in range(4):
                em.do("sync", lambda e, r4=r4: e.dma_start(
                    v_scr[r4 * 128:(r4 + 1) * 128, :], v_st[0][:]),
                    inc=qs, inc_k=16)
            zf_st = qs.n
            for i in range(4):
                em.wait("vector", qs, zf_st)
                em.do("vector", lambda e, i=i: e.memset(
                    v_st[i][:, :].rearrange("p (h x) -> p h x", x=129)[:, :, 128:129],
                    1.0), inc=dve)
            ones_set = dve.n
            em.wait("tensor", qg, w_ld)

            xnf_ready = [0, 0]
            xnf_free = [0, 0]
            kv_free = [zf_st, 0, 0, 0]
            v_free = [zf_st, 0, 0, 0]
            g_free = [0, 0]
            ps_a_free = [0, 0]
            ps_v_free = [0, 0]

            def load_xnf(sb, S, scr, w):
                for kc in range(16):
                    em.do("sync", lambda e, sb=sb, kc=kc, S=S, scr=scr, w=w:
                          e.dma_start_transpose(
                              sb[kc][:, 0:w],
                              scr[S * w:(S + 1) * w, kc * 128:(kc + 1) * 128]),
                          inc=qs, inc_k=16)
                return qs.n

            def proj_block(xt, n_m, w_list, col, stage_mod, dst_fn):
                pa = proj_block.cnt % 2
                proj_block.cnt += 1
                em.wait("tensor", act, ps_a_free[pa])
                for k in range(16):
                    kwargs = dict(start=(k == 0), stop=(k == 15))
                    mmf = (lambda e, k=k, pa=pa, xt=xt, w=w_list, col=col:
                           e.matmul(ps_a[pa][:], w[k][:, col:col + 128], xt[k][:],
                                    **{"start": k == 0, "stop": k == 15}))
                    if k == 15:
                        em.do("tensor", mmf, inc=pe)
                    else:
                        em.do("tensor", mmf)
                mmdone = pe.n
                st = stage_mod % 4
                em.wait("scalar", qs, kv_free[st])
                em.wait("scalar", pe, mmdone)
                kvr = em.do("scalar", lambda e, pa=pa, st=st: e.activation(
                    kv_st[st][:], ps_a[pa][:], AF.Copy), inc=act)
                ps_a_free[pa] = act.n
                em.wait("sync", act, kvr)
                kv_free[st] = em.do("sync", dst_fn(kv_st[st]), inc=qs, inc_k=16)

            proj_block.cnt = 0

            for S in range(8):
                sb2 = S % 2
                em.wait("sync", pe, xnf_free[sb2])
                xnf_ready[sb2] = load_xnf(xnf[sb2], S, xn_scr, 512)
                em.wait("tensor", qs, xnf_ready[sb2])
                xt = xnf[sb2]
                for h in range(8):
                    proj_block(xt, 128, wk_sb, h * 128, h,
                               lambda stg, h=h, S=S: (lambda e: e.dma_start(
                                   k_scr[h * 128:(h + 1) * 128,
                                         512 + S * 512:512 + (S + 1) * 512],
                                   stg[:])))
                for tc in range(4):
                    pv = tc % 2
                    em.wait("tensor", act, ps_v_free[pv])
                    for k in range(16):
                        for hn in range(2):
                            mmf = (lambda e, k=k, tc=tc, pv=pv, hn=hn, xt=xt:
                                   e.matmul(ps_v[pv][:, hn * 512:(hn + 1) * 512],
                                            xt[k][:, tc * 128:(tc + 1) * 128],
                                            wv_sb[k][:, hn * 512:(hn + 1) * 512],
                                            start=(k == 0), stop=(k == 15)))
                            if k == 15 and hn == 1:
                                em.do("tensor", mmf, inc=pe)
                            else:
                                em.do("tensor", mmf)
                    mmv = pe.n
                    vs = tc % 4
                    em.wait("scalar", dve, ones_set)
                    em.wait("scalar", qs, v_free[vs])
                    em.wait("scalar", pe, mmv)
                    vr = em.do("scalar", lambda e, pv=pv, vs=vs: e.activation(
                        v_st[vs][:, :].rearrange("p (h x) -> p h x", x=129)[:, :, 0:128],
                        ps_v[pv][:, :].rearrange("p (h x) -> p h x", x=128),
                        AF.Copy), inc=act)
                    ps_v_free[pv] = act.n
                    em.wait("sync", act, vr)
                    v_free[vs] = em.do("sync", lambda e, vs=vs, S=S, tc=tc: e.dma_start(
                        v_scr[512 + S * 512 + tc * 128:512 + S * 512 + (tc + 1) * 128,
                              :], v_st[vs][:]), inc=qs, inc_k=16)
                gb = S % 2
                em.wait("tensor", dve, g_free[gb])
                for k in range(16):
                    mmf = (lambda e, k=k, xt=xt: e.matmul(
                        ps_g[:], wg_sb[k][:], xt[k][:],
                        start=(k == 0), stop=(k == 15)))
                    em.do("tensor", mmf, inc=pe if k == 15 else None)
                mmg = pe.n
                em.wait("vector", pe, mmg)
                gv = em.do("vector", lambda e, gb=gb: e.tensor_copy(
                    g_st[gb][:], ps_g[:]), inc=dve)
                for tc in range(4):
                    em.wait("tensor", dve, gv)
                    em.wait("tensor", dve, dve.n)
                    tr = em.do("tensor", lambda e, gb=gb, tc=tc: e.transpose(
                        ps_t[:], g_st[gb][:, tc * 128:(tc + 1) * 128],
                        ident_sb[0:8, 0:8]), inc=pe)
                    em.wait("vector", pe, tr)
                    em.do("vector", lambda e, S=S, tc=tc: e.tensor_copy(
                        gpt[:, (S * 4 + tc) * 8:(S * 4 + tc + 1) * 8], ps_t[:]),
                        inc=dve)
                g_free[gb] = dve.n
                xnf_free[sb2] = pe.n

            for So in range(4):
                sb2 = So % 2
                em.wait("sync", pe, xnf_free[sb2])
                xnf_ready[sb2] = load_xnf(xnf[sb2], So, xn_own_scr, 512)
                em.wait("tensor", qs, xnf_ready[sb2])
                xt = xnf[sb2]
                for qd in range(16):
                    proj_block(xt, 128, wq_sb, qd * 128, qd,
                               lambda stg, qd=qd, So=So: (lambda e: e.dma_start(
                                   q_scr[qd * 128:(qd + 1) * 128,
                                         So * 512:(So + 1) * 512], stg[:])))
                xnf_free[sb2] = pe.n

            glast = dve.n
            em.wait("vector", dve, glast)
            zz = em.do("vector", lambda e: e.tensor_scalar(
                zt[:], gpt[:], GATE_OFFSET, None, ALU.add), inc=dve)
            em.wait("scalar", dve, zz)
            az = em.do("scalar", lambda e: e.activation(
                azt[:], zt[:], AF.Abs), inc=act)
            em.wait("vector", act, az)
            ne = em.do("vector", lambda e: e.tensor_scalar(
                zt2[:], azt[:], -1.0, None, ALU.mult), inc=dve)
            em.wait("scalar", dve, ne)
            en = em.do("scalar", lambda e: e.activation(
                azt[:], zt2[:], AF.Exp), inc=act)
            em.wait("vector", act, en)
            en1 = em.do("vector", lambda e: e.tensor_scalar(
                azt[:], azt[:], 1.0, None, ALU.add), inc=dve)
            em.wait("scalar", dve, en1)
            l1 = em.do("scalar", lambda e: e.activation(
                azt[:], azt[:], AF.Ln), inc=act)
            em.wait("vector", act, l1)
            em.do("vector", lambda e: e.scalar_tensor_tensor(
                gpt[:], zt[:], 0.0, azt[:], ALU.min, ALU.subtract), inc=dve)
            lgr = dve.n
            em.wait("tensor", dve, lgr)
            for blk in range(32):
                em.do("tensor", lambda e, blk=blk: e.matmul(
                    ps_t[:], ublk_sb[:], gpt[:, blk * 8:(blk + 1) * 8],
                    start=True, stop=True), inc=pe)
                em.wait("vector", pe, pe.n)
                em.do("vector", lambda e, blk=blk: e.tensor_copy(
                    logg_pt[:, blk * 8:(blk + 1) * 8], ps_t[:]), inc=dve)
                em.wait("tensor", dve, dve.n)
            cum_done = dve.n
            em.wait("sync", dve, cum_done)
            r127 = em.do("sync", lambda e: e.dma_start(
                tmp_scr[:, :].rearrange("(o b) h -> o (b h)", o=1),
                logg_pt[127:128, :]), inc=qs, inc_k=16)
            em.wait("sync", qs, r127)
            bs = em.do("sync", lambda e: e.dma_start(
                bsum[:, :], tmp_scr[:, :]), inc=qs, inc_k=16)
            em.wait("tensor", qs, bs)
            for blk in range(1, 32):
                em.do("tensor", lambda e, blk=blk: e.matmul(
                    ps_t[:], ones32_sb[0:blk, :], bsum[0:blk, :],
                    start=True, stop=True), inc=pe)
                em.wait("vector", pe, pe.n)
                em.do("vector", lambda e, blk=blk: e.tensor_tensor(
                    logg_pt[:, blk * 8:(blk + 1) * 8],
                    logg_pt[:, blk * 8:(blk + 1) * 8], ps_t[:], ALU.add), inc=dve)
                em.wait("tensor", dve, dve.n)
            logg_done = dve.n
            if "logg_scr" in dbg:
                em.wait("sync", dve, logg_done)
                em.do("sync", lambda e: e.dma_start(logg_scr[:, :], logg_pt[:]),
                      inc=qs, inc_k=16)
            c_all, eps_all = zt, azt
            em.wait("scalar", dve, logg_done)
            em.do("scalar", lambda e: e.activation(
                c_all[:], logg_pt[:], AF.Exp, scale=-0.5, bias=cc("c_bias")), inc=act)
            ea = em.do("scalar", lambda e: e.activation(
                eps_all[:], logg_pt[:], AF.Exp, scale=-1.0, bias=cc("eps_bias")),
                inc=act)
            zp = em.do("vector", lambda e: e.memset(c_virt[:, 0:64], 0.0), inc=dve)
            em.wait("sync", dve, zp)
            em.wait("sync", act, ea)
            em.do("sync", lambda e: e.dma_start(cg_scr[:, 0:32], c_virt[:, 0:32]),
                  inc=qs, inc_k=16)
            em.do("sync", lambda e: e.dma_start(cg_scr[:, 288:320], c_virt[:, 32:64]),
                  inc=qs, inc_k=16)
            em.do("sync", lambda e: e.dma_start(cg_scr[:, 32:288], c_all[:]),
                  inc=qs, inc_k=16)
            st_t = em.do("sync", lambda e: e.dma_start(eps_scr[:, :], eps_all[:]),
                         inc=qs, inc_k=16)
            em.wait("sync", qs, st_t)
            em.do("sync", dyn_gather(c_virt[:, :], cg_scr, J_CV, VBLK * 8, 32),
                  inc=qs, inc_k=16)
            for ob in range(4):
                em.do("sync", dyn_gather(
                    eps_own[:, ob * 32:(ob + 1) * 32],
                    eps_scr[:, 64 * ob:64 * ob + 64], J_EPS, 32, 32),
                    inc=qs, inc_k=16)
            em.replay(nc)

        if max_phase < 3:
            return nc
        with ExitStack() as ph:
            pp = ph.enter_context
            em = Emitter()
            barrier(em)
            xnfb = [pp(nc.sbuf_tensor(f"p1xn{k}", [128, 1024], BF16)) for k in range(16)]
            hdn = [pp(nc.sbuf_tensor(f"hdn{i}", [128, 1024], BF16)) for i in range(NI)]
            gw_b = [pp(nc.sbuf_tensor(f"gw{i}", [128, 2048], BF16)) for i in range(2)]
            uw_b = [pp(nc.sbuf_tensor(f"uw{i}", [128, 2048], BF16)) for i in range(2)]
            sg_b = [pp(nc.sbuf_tensor(f"sg{i}", [128, 1024], BF16)) for i in range(2)]
            uu_b = [pp(nc.sbuf_tensor(f"uu{i}", [128, 1024], BF16)) for i in range(2)]
            dw_ch = [pp(nc.sbuf_tensor(f"dwch{i}", [128, 11 * 512], BF16))
                     for i in range(2)]
            gst_sb = pp(nc.sbuf_tensor("gst_sb", [128, NI], F32))
            ust_sb = pp(nc.sbuf_tensor("ust_sb", [128, NI], F32))
            dsb_row = pp(nc.sbuf_tensor("dsb_row", [1, C], F32))
            ds_bc = pp(nc.sbuf_tensor("ds_bc", [128, C], F32))
            t1_b = [pp(nc.sbuf_tensor(f"t1_{i}", [128, 512], F32)) for i in range(2)]
            go_b = [pp(nc.sbuf_tensor(f"go_{i}", [128, 512], F32)) for i in range(2)]

            ps_gu = [pp(nc.psum_tensor(f"ps_gu{i}", [128, 1024], F32)) for i in range(4)]

            em.do("sync", lambda e: e.dma_start(gst_sb[:], gate_st[:, :]),
                  inc=qs, inc_k=16)
            em.do("sync", lambda e: e.dma_start(ust_sb[:], up_st[:, :]),
                  inc=qs, inc_k=16)
            em.do("sync", lambda e: e.dma_start(dsb_row[:], down_s[:, :]),
                  inc=qs, inc_k=16)
            sc_ld = qs.n
            em.wait("tensor", qs, sc_ld)
            for rnd in range(2):
                for hn in range(4):
                    em.do("tensor", lambda e, hn=hn, rnd=rnd: e.matmul(
                        ps_gu[0][:, hn * 256:(hn + 1) * 256], ones1f[:, :],
                        dsb_row[:, rnd * 1024 + hn * 256:rnd * 1024 + (hn + 1) * 256],
                        start=True, stop=True), inc=pe)
                em.wait("vector", pe, pe.n)
                em.do("vector", lambda e, rnd=rnd: e.tensor_copy(
                    ds_bc[:, rnd * 1024:(rnd + 1) * 1024], ps_gu[0][:]), inc=dve)
                em.wait("tensor", dve, dve.n)
            dsbc_done = dve.n

            for S in range(2 if P1_SUB >= 1 else 0):
                em.wait("sync", pe, pe.n)
                for kc in range(16):
                    em.do("sync", lambda e, kc=kc, S=S: e.dma_start_transpose(
                        xnfb[kc][:], xn_own_scr[S * 1024:(S + 1) * 1024,
                                                kc * 128:(kc + 1) * 128]),
                        inc=qs, inc_k=16)
                xn_ld = qs.n
                em.wait("tensor", qs, xn_ld)
                em.wait("tensor", dve, dsbc_done)
                em.wait("tensor", act, act.n)

                gw_free = [0] * 2
                hdn_done = 0
                for i in range(NI if P1_GRAN >= 2 else 0):
                    bw = i % 2
                    em.wait("gpsimd", pe, gw_free[bw])
                    em.do("gpsimd", lambda e, bw=bw, i=i: e.dma_start(
                        gw_b[bw][:, :].rearrange("p (k c) -> p k c", c=128),
                        gate_wt[:, i * 128:(i + 1) * 128].rearrange(
                            "(k p) c -> p k c", p=128)), inc=qg, inc_k=16)
                    em.do("gpsimd", lambda e, bw=bw, i=i: e.dma_start(
                        uw_b[bw][:, :].rearrange("p (k c) -> p k c", c=128),
                        up_wt[:, i * 128:(i + 1) * 128].rearrange(
                            "(k p) c -> p k c", p=128)), inc=qg, inc_k=16)
                    w_rdy = qg.n
                    pg, pu = ps_gu[(i % 2) * 2], ps_gu[(i % 2) * 2 + 1]
                    if P1_GRAN < 3:
                        continue
                    em.wait("tensor", qg, w_rdy)
                    em.wait("tensor", act, getattr(em, f"_p1sg{i%2}", 0) or 0)
                    em.wait("tensor", dve, getattr(em, f"_p1uu{i%2}", 0) or 0)
                    for k in range(16):
                        for hn in range(2):
                            mmf = (lambda e, k=k, hn=hn, bw=bw, pg=pg: e.matmul(
                                pg[:, hn * 512:(hn + 1) * 512],
                                gw_b[bw][:, k * 128:(k + 1) * 128],
                                xnfb[k][:, hn * 512:(hn + 1) * 512],
                                start=(k == 0), stop=(k == 15)))
                            em.do("tensor", mmf,
                                  inc=pe if (k == 15 and hn == 1) else None)
                    g_mm = pe.n
                    for k in range(16):
                        for hn in range(2):
                            mmf = (lambda e, k=k, hn=hn, bw=bw, pu=pu: e.matmul(
                                pu[:, hn * 512:(hn + 1) * 512],
                                uw_b[bw][:, k * 128:(k + 1) * 128],
                                xnfb[k][:, hn * 512:(hn + 1) * 512],
                                start=(k == 0), stop=(k == 15)))
                            em.do("tensor", mmf,
                                  inc=pe if (k == 15 and hn == 1) else None)
                    u_mm = pe.n
                    gw_free[bw] = pe.n
                    if P1_GRAN < 4:
                        continue
                    sgb, uub = sg_b[i % 2], uu_b[i % 2]
                    em.wait("scalar", pe, g_mm)
                    em.wait("scalar", dve, hdn_done)
                    sg_done = em.do("scalar", lambda e, sgb=sgb, pg=pg, i=i:
                        e.activation(sgb[:], pg[:], AF.Silu,
                                     scale=gst_sb[:, i:i + 1]), inc=act)
                    setattr(em, f"_p1sg{i%2}", act.n)
                    if "dbg_scr" in dbg and S == 0 and i == 0:
                        em.wait("vector", pe, g_mm)
                        pr = em.do("vector", lambda e, pg=pg: e.tensor_copy(
                            t1_b[0][:], pg[:, 0:512]), inc=dve)
                        em.wait("sync", dve, pr)
                        em.do("sync", lambda e: e.dma_start(
                            dbg_scr[:, 0:512], t1_b[0][:]), inc=qs, inc_k=16)
                        em.do("sync", lambda e: e.dma_start(
                            dbg_scr[0:128, 512:512 + NI].rearrange("p c -> p c"),
                            gst_sb[:, :]), inc=qs, inc_k=16)
                    if P1_GRAN < 5:
                        continue
                    em.wait("vector", pe, u_mm)
                    em.do("vector", lambda e, uub=uub, pu=pu, i=i: e.tensor_scalar(
                        uub[:], pu[:], ust_sb[:, i:i + 1], None, ALU.mult), inc=dve)
                    setattr(em, f"_p1uu{i%2}", dve.n)
                    if P1_GRAN < 6:
                        continue
                    em.wait("vector", act, sg_done)
                    hdn_done = em.do("vector", lambda e, i=i, sgb=sgb, uub=uub:
                        e.tensor_mul(hdn[i][:], sgb[:], uub[:]), inc=dve)
                    if "hdn_scr" in dbg and S == 0:
                        em.wait("sync", dve, hdn_done)
                        em.do("sync", lambda e, i=i: e.dma_start(
                            hdn_scr[i, :, :], hdn[i][:]), inc=qs, inc_k=16)
                        if i < 2:
                            em.do("sync", lambda e, i=i, sgb=sgb: e.dma_start(
                                sg_scr[i, :, :], sgb[:]), inc=qs, inc_k=16)
                            em.do("sync", lambda e, i=i, uub=uub: e.dma_start(
                                sg_scr[2 + i, :, :], uub[:]), inc=qs, inc_k=16)

                hdn_all = dve.n
                em.wait("tensor", dve, hdn_all)
                em.wait("tensor", dve, dsbc_done)
                go_free = [0] * 2
                t1_free = [0] * 2
                dw_free = [0] * 2
                ps_d = ps_gu
                for cs in range(4 if P1_SUB >= 2 else 0):
                    for th in range(2):
                        for ic in range(4):
                            bw2 = ic % 2
                            em.wait("gpsimd", pe, dw_free[bw2])
                            dwl = em.do("gpsimd", lambda e, cs=cs, ic=ic, bw2=bw2:
                                e.dma_start(
                                    dw_ch[bw2][:, :].rearrange(
                                        "p (i c) -> p i c", c=512),
                                    down_wt[ic * 1408:(ic + 1) * 1408,
                                            cs * 512:(cs + 1) * 512].rearrange(
                                        "(i p) c -> p i c", p=128)),
                                inc=qg, inc_k=16)
                            em.wait("tensor", qg, dwl)
                            for il in range(11):
                                i = ic * 11 + il
                                for tc in range(4):
                                    pd = ps_d[tc // 2][:, (tc % 2) * 512:
                                                       (tc % 2 + 1) * 512]
                                    mmf = (lambda e, i=i, il=il, tc=tc, th=th,
                                           pd=pd, bw2=bw2: e.matmul(
                                        pd, hdn[i][:, (th * 4 + tc) * 128:
                                                   (th * 4 + tc + 1) * 128],
                                        dw_ch[bw2][:, il * 512:(il + 1) * 512],
                                        start=(i == 0), stop=(i == NI - 1)))
                                    em.do("tensor", mmf,
                                          inc=pe if (il == 10 and tc == 3) else None)
                            dw_free[bw2] = pe.n
                        d_mm = pe.n
                        for tc in range(4):
                            pd = ps_d[tc // 2][:, (tc % 2) * 512:(tc % 2 + 1) * 512]
                            t1 = t1_b[tc % 2]
                            gob = go_b[tc % 2]
                            em.wait("vector", pe, d_mm)
                            em.wait("vector", act, t1_free[tc % 2])
                            d1 = em.do("vector", lambda e, pd=pd, t1=t1, cs=cs:
                                e.tensor_tensor(t1[:], pd,
                                                ds_bc[:, cs * 512:(cs + 1) * 512],
                                                ALU.mult), inc=dve)
                            em.wait("scalar", dve, d1)
                            em.wait("scalar", qs, go_free[tc % 2])
                            gs = em.do("scalar", lambda e, t1=t1, gob=gob:
                                e.activation(gob[:], t1[:], AF.Silu), inc=act)
                            t1_free[tc % 2] = act.n
                            em.wait("sync", act, gs)
                            go_free[tc % 2] = em.do("sync",
                                lambda e, gob=gob, S=S, th=th, tc=tc, cs=cs:
                                e.dma_start(
                                    gsig_scr[S * 1024 + th * 512 + tc * 128:
                                             S * 1024 + th * 512 + (tc + 1) * 128,
                                             cs * 512:(cs + 1) * 512], gob[:]),
                                inc=qs, inc_k=16)
                        em.wait("tensor", dve, dve.n)
            em.replay(nc)

        if max_phase < 4:
            return nc
        with ExitStack() as ph:
            pp = ph.enter_context
            em = Emitter()
            barrier(em)
            k_res = pp(nc.sbuf_tensor("kres", [128, 8 * VBLK * 128], BF16))
            v_res = pp(nc.sbuf_tensor("vres", [128, VBLK * 1032], BF16))
            q_b = [pp(nc.sbuf_tensor(f"qb{i}", [128, 512], BF16)) for i in range(2)]
            ab = [pp(nc.sbuf_tensor(f"ab{i}", [128, 512], BF16)) for i in range(4)]
            tmp_bf = pp(nc.sbuf_tensor("tmp_bf", [128, 512], BF16))
            yn_b = [pp(nc.sbuf_tensor(f"ynb{i}", [128, 128], BF16)) for i in range(4)]
            ep_b = pp(nc.sbuf_tensor("ep_b", [128, 8], F32))

            py_dbg = pp(nc.sbuf_tensor("py_dbg", [128, 258], F32))
            ps_qk = [pp(nc.psum_tensor(f"ps_qk{i}", [128, 512], F32)) for i in range(2)]
            py = [pp(nc.psum_tensor(f"py{i}", [128, 129], F32)) for i in range(4)]

            def kload(e):
                with e.register(_rn()) as r:
                    e.reg_load(r, dyn_sb[0:1, J_KOFF:J_KOFF + 1])
                    off = e.snap(r, donate=True, min_val=0, max_val=512)
                    src = k_scr[:, ds(off, VBLK * 128)].rearrange(
                        "(h p) c -> p h c", p=128)
                    return e.dma_start(
                        k_res[:, :].rearrange("p (h c) -> p h c", c=VBLK * 128),
                        src)

            def vload(e):
                with e.register(_rn()) as r:
                    e.reg_load(r, dyn_sb[0:1, J_VOFF:J_VOFF + 1])
                    off = e.snap(r, donate=True, min_val=0, max_val=512)
                    src = v_scr[ds(off, VBLK * 128), :].rearrange(
                        "(b p) x -> p b x", p=128)
                    return e.dma_start(
                        v_res[:, :].rearrange("p (b x) -> p b x", x=1032),
                        src)

            em.do("sync", kload, inc=qs, inc_k=16)
            em.do("sync", vload, inc=qs, inc_k=16)
            kv_ld = qs.n
            em.wait("tensor", qs, kv_ld)

            q_ready = [0, 0]
            q_free = [0, 0]

            def q_load(idx, os, h):
                em.wait("sync", pe, q_free[idx])
                q_ready[idx] = em.do("sync", lambda e, os=os, h=h, idx=idx:
                    e.dma_start(q_b[idx][:],
                                q_scr[h * 128:(h + 1) * 128,
                                      os * 512:(os + 1) * 512]), inc=qs, inc_k=16)

            jobs = [(os, h) for os in range(4) for h in range(16)]
            q_load(0, *jobs[0])
            ep_act, ep_dve = 0, 0
            bank_evac = [None, None]
            abuf_mm2 = [0] * 4

            for ji, (os, h) in enumerate(jobs):
                kbmax = 8 * os + 8
                hv = h // 2
                qi = ji % 2
                if ji + 1 < len(jobs):
                    q_load(1 - qi, *jobs[ji + 1])
                em.wait("tensor", qs, q_ready[qi])
                evac_done = {}

                def emit_mm2(kb, kbmax=None, hv=None, evac_done=None):
                    j = kb - (kbmax - 4)
                    tcs = range(max(j, 0), 4)
                    etl, ecnt = evac_done[kb]
                    em.wait("tensor", etl, ecnt)
                    last_tc = max(tcs)
                    for tc in tcs:
                        stop = (kb == kbmax - 4 + tc)
                        pyt = py[tc][:, 0:129]
                        mmf = (lambda e, kb=kb, tc=tc, pyt=pyt, stop=stop, hv=hv:
                               e.matmul(pyt, ab[kb % 4][:, tc * 128:(tc + 1) * 128],
                                        v_res[:, kb * 1032 + hv * 129:
                                              kb * 1032 + hv * 129 + 129],
                                        start=(kb == 0), stop=stop))
                        em.do("tensor", mmf, inc=pe if tc == last_tc else None)
                    abuf_mm2[kb % 4] = pe.n

                for kb in range(kbmax):
                    j = kb - (kbmax - 4)
                    pq = kb % 2
                    if bank_evac[pq] is not None:
                        etl, ecnt = bank_evac[pq]
                        em.wait("tensor", etl, ecnt)
                    if kb == 0:
                        em.wait("tensor", act, ep_act)
                        em.wait("tensor", dve, ep_dve)
                    n0 = j * 128 if j > 0 else 0
                    kcol = hv * VBLK * 128 + kb * 128
                    mmf = (lambda e, kb=kb, pq=pq, n0=n0, kcol=kcol, qi=qi: e.matmul(
                        ps_qk[pq][:, n0:512], k_res[:, kcol:kcol + 128],
                        q_b[qi][:, n0:512], start=True, stop=True))
                    em.do("tensor", mmf, inc=pe)
                    mm1_cnt = pe.n
                    a_war = abuf_mm2[kb % 4]
                    ccol = c_virt[:, kb * 8 + hv:kb * 8 + hv + 1]
                    if kb % 2 == 0:
                        em.wait("scalar", pe, mm1_cnt)
                        if a_war:
                            em.wait("scalar", pe, a_war)
                        ev = em.do("scalar", lambda e, kb=kb, pq=pq, ccol=ccol:
                            e.activation(ab[kb % 4][:], ps_qk[pq][:], AF.Square,
                                         scale=ccol), inc=act)
                        evac_done[kb] = (act, ev)
                        bank_evac[pq] = (act, ev)
                    else:
                        em.wait("vector", pe, mm1_cnt)
                        if a_war:
                            em.wait("vector", pe, a_war)
                        ts_cnt = em.do("vector", lambda e, pq=pq, ccol=ccol:
                            e.tensor_scalar(
                            tmp_bf[:], ps_qk[pq][:], ccol, None, ALU.mult), inc=dve)
                        bank_evac[pq] = (dve, ts_cnt)
                        ev = em.do("vector", lambda e, kb=kb: e.tensor_mul(
                            ab[kb % 4][:], tmp_bf[:], tmp_bf[:]), inc=dve)
                        evac_done[kb] = (dve, ev)
                    if j >= 0:
                        etl, ecnt = evac_done[kb]
                        em.wait("vector", etl, ecnt)
                        mk = em.do("vector", lambda e, kb=kb, j=j: e.tensor_mul(
                            ab[kb % 4][:, j * 128:(j + 1) * 128],
                            ab[kb % 4][:, j * 128:(j + 1) * 128], tri_sb[:]),
                            inc=dve)
                        evac_done[kb] = (dve, mk)
                    if kb >= 1:
                        emit_mm2(kb - 1, kbmax=kbmax, hv=hv, evac_done=evac_done)
                emit_mm2(kbmax - 1, kbmax=kbmax, hv=hv, evac_done=evac_done)
                q_free[qi] = pe.n

                all_mm2 = pe.n
                em.wait("vector", pe, all_mm2)
                em.wait("vector", act, ep_act)
                em.wait("scalar", pe, all_mm2)
                for tc in range(4):
                    oc = os * 4 + tc
                    pyt = py[tc]
                    base = 0
                    na = em.do("scalar", lambda e, pyt=pyt, base=base, tc=tc,
                               oc=oc, hv=hv:
                        e.add(ep_b[:, tc:tc + 1],
                              pyt[:, base + 128:base + 129],
                              eps_own[:, oc * 8 + hv:oc * 8 + hv + 1]), inc=act)
                    em.wait("vector", act, na)
                    rr = em.do("vector", lambda e, tc=tc: e.reciprocal(
                        ep_b[:, 4 + tc:5 + tc], ep_b[:, tc:tc + 1]), inc=dve)
                    em.wait("scalar", dve, rr)
                    em.wait("scalar", qs, getattr(em, f"_ynfree{tc}", 0) or 0)
                    yv = em.do("scalar", lambda e, pyt=pyt, base=base, tc=tc:
                        e.activation(yn_b[tc][:], pyt[:, base:base + 128], AF.Copy,
                                     scale=ep_b[:, 4 + tc:5 + tc]), inc=act)
                    em.wait("sync", act, yv)
                    st = em.do("sync", lambda e, tc=tc, os=os, h=h: e.dma_start(
                        y_scr[(os * 4 + tc) * 128:(os * 4 + tc + 1) * 128,
                              h * 128:(h + 1) * 128], yn_b[tc][:]), inc=qs, inc_k=16)
                    setattr(em, f"_ynfree{tc}", st)
                ep_act, ep_dve = act.n, dve.n
            em.replay(nc)

        if max_phase < 5:
            return nc
        with ExitStack() as ph:
            pp = ph.enter_context
            em = Emitter()
            barrier(em)
            wc_full = pp(nc.sbuf_tensor("wc_full", [128, 16 * 2048], BF16))
            yf = [pp(nc.sbuf_tensor(f"yf{q}", [128, 512], BF16)) for q in range(16)]
            bc_row = pp(nc.sbuf_tensor("bc_row", [1, C], F32))
            bc_bc = pp(nc.sbuf_tensor("bc_bc", [128, C], F32))
            xg_b = [pp(nc.sbuf_tensor(f"xg{i}", [128, 512], F32)) for i in range(4)]
            gg_b = [pp(nc.sbuf_tensor(f"gg{i}", [128, 512], F32)) for i in range(4)]
            o_b = [pp(nc.sbuf_tensor(f"ob{i}", [128, 512], F32)) for i in range(4)]
            t2_b = [pp(nc.sbuf_tensor(f"t2_{i}", [128, 512], F32)) for i in range(2)]
            ps_s = [pp(nc.psum_tensor(f"ps_s{i}", [128, 512], F32)) for i in range(4)]
            ps_b = pp(nc.psum_tensor("ps_b", [128, 512], F32))

            em.do("gpsimd", lambda e: e.dma_start(
                wc_full[:, :].rearrange("p (q c) -> p q c", c=2048),
                wc[:, :].rearrange("(q p) c -> p q c", p=128)),
                inc=qg, inc_k=16)
            em.do("sync", lambda e: e.dma_start(bc_row[:], bc[:, :]), inc=qs, inc_k=16)
            bc_ld = qs.n
            wc_ld = qg.n
            em.wait("tensor", qs, bc_ld)
            for hn in range(4):
                em.do("tensor", lambda e, hn=hn: e.matmul(
                    ps_b[:], ones1f[:, :], bc_row[:, hn * 512:(hn + 1) * 512],
                    start=True, stop=True), inc=pe)
                em.wait("vector", pe, pe.n)
                em.do("vector", lambda e, hn=hn: e.tensor_copy(
                    bc_bc[:, hn * 512:(hn + 1) * 512], ps_b[:]), inc=dve)
                em.wait("tensor", dve, dve.n)
            em.wait("tensor", qg, wc_ld)

            xg_free = [0] * 4
            o_free = [0] * 4
            for os in range(4):
                yld0 = qs.n
                for qd in range(16):
                    em.do("sync", lambda e, qd=qd, os=os: e.dma_start_transpose(
                        yf[qd][:], y_scr[os * 512:(os + 1) * 512,
                                         qd * 128:(qd + 1) * 128]),
                        inc=qs, inc_k=16)
                y_ld = qs.n
                em.wait("tensor", qs, y_ld)
                for cs in range(4):
                    xg_ld = [0] * 4
                    for tc in range(4):
                        em.wait("sync", dve, xg_free[tc])
                        em.do("sync", lambda e, tc=tc, os=os, cs=cs: e.dma_start(
                            xg_b[tc][:], x_own[os * 512 + tc * 128:
                                               os * 512 + (tc + 1) * 128,
                                               cs * 512:(cs + 1) * 512]),
                            inc=qs, inc_k=16)
                        xg_ld[tc] = em.do("sync",
                            lambda e, tc=tc, os=os, cs=cs: e.dma_start(
                                gg_b[tc][:], gsig_scr[os * 512 + tc * 128:
                                                      os * 512 + (tc + 1) * 128,
                                                      cs * 512:(cs + 1) * 512]),
                            inc=qs, inc_k=16)
                    for qd in range(16):
                        for tc in range(4):
                            mmf = (lambda e, qd=qd, tc=tc, cs=cs: e.matmul(
                                ps_s[tc][:], yf[qd][:, tc * 128:(tc + 1) * 128],
                                wc_full[:, qd * 2048 + cs * 512:
                                        qd * 2048 + (cs + 1) * 512],
                                start=(qd == 0), stop=(qd == 15)))
                            em.do("tensor", mmf,
                                  inc=pe if (qd == 15 and tc == 3) else None)
                    s_mm = pe.n
                    for tc in range(4):
                        t2 = t2_b[tc % 2]
                        em.wait("vector", pe, s_mm)
                        em.wait("vector", qs, xg_ld[tc])
                        em.do("vector", lambda e, tc=tc, t2=t2, cs=cs:
                            e.tensor_tensor(t2[:], ps_s[tc][:],
                                            bc_bc[:, cs * 512:(cs + 1) * 512],
                                            ALU.add), inc=dve)
                        em.do("vector", lambda e, tc=tc, t2=t2: e.tensor_mul(
                            t2[:], t2[:], gg_b[tc][:]), inc=dve)
                        em.wait("vector", qs, o_free[tc])
                        ov = em.do("vector", lambda e, tc=tc, t2=t2: e.tensor_add(
                            o_b[tc][:], t2[:], xg_b[tc][:]), inc=dve)
                        xg_free[tc] = dve.n
                        em.wait("sync", dve, ov)
                        o_free[tc] = em.do("sync",
                            lambda e, tc=tc, os=os, cs=cs: e.dma_start(
                                out_own[os * 512 + tc * 128:
                                        os * 512 + (tc + 1) * 128,
                                        cs * 512:(cs + 1) * 512], o_b[tc][:]),
                            inc=qs, inc_k=16)
                    em.wait("tensor", dve, dve.n)
            em.wait("sync", qs, qs.n)
            em.replay(nc)

    return nc


_BF = ml_dtypes.bfloat16


def _host_inputs(core, full):
    b, half = core // 2, core % 2
    x = full["L0_attn_hidden_states"]
    xb = np.ascontiguousarray(x[b]).astype(np.float32, copy=False)
    own = np.concatenate([xb[1024 * os + 512 * half:1024 * os + 512 * half + 512]
                          for os in range(4)], axis=0)
    shiftb = 4 * (1 - half)
    dyn = np.zeros((1, 96), dtype=np.int32)
    dyn[0, J_KOFF] = 512 - shiftb * 128
    dyn[0, J_VOFF] = 512 - shiftb * 128
    dyn[0, J_CV] = 32 - shiftb * 8
    dyn[0, J_EPS] = 32 * half
    consts_arr = np.zeros((128, len(CONST_COLS)), dtype=np.float32)
    for i, k in enumerate(CONST_COLS):
        consts_arr[:, i] = CONSTS[k]
    d = {
        "x_all": xb,
        "x_own": np.ascontiguousarray(own),
        "wq": full["L0_attn_Wq"], "wk": full["L0_attn_Wk"],
        "wv": full["L0_attn_Wv"], "wg": full["L0_attn_Wg"],
        "wc": full["L0_attn_Wc"],
        "bc": full["L0_attn_bc"].reshape(1, C),
        "gate_wt": full["_gate_wt"], "up_wt": full["_up_wt"],
        "down_wt": full["_down_wt"],
        "gate_st": full["_gate_st"], "up_st": full["_up_st"],
        "down_s": full["L0_ffn_down_s"].reshape(1, C),
        "tri": full["_tri"], "ublk": full["_ublk"], "ones32": full["_ones32"],
        "ident": full["_ident"], "consts": consts_arr, "dyn": dyn,
    }
    return d


def _prep_shared(inputs):
    full = dict(inputs)
    for k in list(full.keys()):
        full[k] = np.asarray(full[k])
    full["_gate_wt"] = np.ascontiguousarray(full["L0_ffn_gate_w"].T).astype(np.float32)
    full["_up_wt"] = np.ascontiguousarray(full["L0_ffn_up_w"].T).astype(np.float32)
    full["_down_wt"] = np.ascontiguousarray(full["L0_ffn_down_w"].T).astype(np.float32)
    full["_gate_st"] = np.ascontiguousarray(
        full["L0_ffn_gate_s"].reshape(NI, 128).T).astype(np.float32)
    full["_up_st"] = np.ascontiguousarray(
        full["L0_ffn_up_s"].reshape(NI, 128).T).astype(np.float32)
    full["_tri"] = np.triu(np.ones((128, 128), dtype=np.float32)).astype(_BF)
    full["_ublk"] = np.triu(np.ones((128, 128), dtype=np.float32))
    full["_ones32"] = np.ones((32, 128), dtype=np.float32)
    full["_ident"] = np.eye(128, dtype=np.float32)
    return full


def kernel(**inputs):
    full = _prep_shared(inputs)
    nc = build_program()
    in_maps = [_host_inputs(c, full) for c in range(8)]
    res = run_bass_kernel_spmd(nc, in_maps, core_ids=list(range(8)))
    out = np.zeros((B, T, C), dtype=np.float32)
    for core in range(8):
        b, half = core // 2, core % 2
        oo = res.results[core]["out_own"]
        for os in range(4):
            out[b, 1024 * os + 512 * half:1024 * os + 512 * half + 512] = \
                oo[os * 512:(os + 1) * 512]
    return out


# version 02
# speedup vs baseline: 1.0461x; 1.0461x over previous
import numpy as np
import ml_dtypes
from contextlib import ExitStack

import concourse.bass as bass
import concourse.mybir as mybir
from concourse.bass_utils import run_bass_kernel_spmd

F32 = mybir.dt.float32
BF16 = mybir.dt.bfloat16
I32 = mybir.dt.int32
AF = mybir.ActivationFunctionType
ALU = mybir.AluOpType
AX = mybir.AxisListType
ds = bass.ds

B, T, C = 4, 4096, 2048
H, D = 8, 128
QD, KD = 2048, 1024
INTER = 5632
NI = INTER // 128
GATE_OFFSET = 6.906768
OWN = 2048
NBLK = T // 128
VBLK = NBLK + 4
C_BIAS = -0.5 * float(np.log(128.0))
EPS_BIAS = float(np.log(1e-6))

CONSTS = {
    "c_bias": C_BIAS, "eps_bias": EPS_BIAS, "ln_eps": 1e-5, "one": 1.0,
    "go": GATE_OFFSET, "zero": 0.0, "inv_c": 1.0 / 2048.0, "neg1": -1.0,
}
CONST_COLS = list(CONSTS.keys())

J_KOFF, J_VOFF, J_CV, J_EPS = 0, 1, 2, 3
P1_SUB = 2
P1_GRAN = 6
P2_EVAC = "act"


class TL:
    def __init__(self, name):
        self.name, self.n, self.sem = name, 0, None


class Emitter:
    ENGS = ("tensor", "vector", "scalar", "gpsimd", "sync")

    def __init__(self):
        self.ops = {k: [] for k in self.ENGS}

    def wait(self, eng, tl, n):
        if n and n > 0:
            self.ops[eng].append(lambda e, s=tl, v=n: e.wait_ge(s.sem, v))

    def do(self, eng, f, inc=None, inc_k=1):
        if inc is None:
            self.ops[eng].append(f)
            return None
        inc.n += inc_k
        self.ops[eng].append(lambda e, f=f, s=inc, k=inc_k: f(e).then_inc(s.sem, k))
        return inc.n

    def replay(self, nc):
        with nc.Block() as blk:
            for name in self.ENGS:
                oplist = self.ops[name]
                if not oplist:
                    continue

                def body(e, oplist=oplist):
                    for f in oplist:
                        f(e)
                getattr(blk, name)(body)


_REGC = [0]
def _rn():
    _REGC[0] += 1
    return f"dynr{_REGC[0]}"


def build_program(debug=(), max_phase=5, detect_races=True):
    nc = bass.Bass("TRN2", target_bir_lowering=False,
                   detect_race_conditions=detect_races)
    dbg = set(debug)

    def scratch(name, shape, dt):
        return nc.dram_tensor(name, shape, dt,
                              kind="ExternalOutput" if name in dbg else "Internal")

    x_all = nc.dram_tensor("x_all", [T, C], F32, kind="ExternalInput")
    x_own = nc.dram_tensor("x_own", [OWN, C], F32, kind="ExternalInput")
    wq = nc.dram_tensor("wq", [C, QD], F32, kind="ExternalInput")
    wk = nc.dram_tensor("wk", [C, KD], F32, kind="ExternalInput")
    wv = nc.dram_tensor("wv", [C, KD], F32, kind="ExternalInput")
    wg = nc.dram_tensor("wg", [C, H], F32, kind="ExternalInput")
    wc = nc.dram_tensor("wc", [QD, C], F32, kind="ExternalInput")
    bc = nc.dram_tensor("bc", [1, C], F32, kind="ExternalInput")
    gate_wt = nc.dram_tensor("gate_wt", [C, INTER], F32, kind="ExternalInput")
    up_wt = nc.dram_tensor("up_wt", [C, INTER], F32, kind="ExternalInput")
    down_wt = nc.dram_tensor("down_wt", [INTER, C], F32, kind="ExternalInput")
    gate_st = nc.dram_tensor("gate_st", [128, NI], F32, kind="ExternalInput")
    up_st = nc.dram_tensor("up_st", [128, NI], F32, kind="ExternalInput")
    down_s = nc.dram_tensor("down_s", [1, C], F32, kind="ExternalInput")
    tri = nc.dram_tensor("tri", [128, 128], BF16, kind="ExternalInput")
    ublk = nc.dram_tensor("ublk", [128, 128], F32, kind="ExternalInput")
    ones32 = nc.dram_tensor("ones32", [32, 128], F32, kind="ExternalInput")
    ident = nc.dram_tensor("ident", [128, 128], F32, kind="ExternalInput")
    consts = nc.dram_tensor("consts", [128, len(CONST_COLS)], F32, kind="ExternalInput")
    dyn = nc.dram_tensor("dyn", [1, 96], I32, kind="ExternalInput")

    out_own = nc.dram_tensor("out_own", [OWN, C], F32, kind="ExternalOutput")

    xn_scr = scratch("xn_scr", [T, C], BF16)
    xn_own_scr = scratch("xn_own_scr", [OWN, C], BF16)
    q_scr = scratch("q_scr", [QD, OWN], BF16)
    k_scr = scratch("k_scr", [KD, T + 1024], BF16)
    v_scr = scratch("v_scr", [T + 1024, 1032], BF16)
    y_scr = scratch("y_scr", [OWN, QD], BF16)
    gsig_scr = scratch("gsig_scr", [OWN, C], F32)
    tmp_scr = scratch("tmp_scr", [32, 8], F32)
    logg_scr = scratch("logg_scr", [128, 256], F32)
    cg_scr = scratch("cg_scr", [128, 320], F32)
    eps_scr = scratch("eps_scr", [128, 256], F32)
    hdn_scr = scratch("hdn_scr", [NI, 128, 1024], BF16)
    sg_scr = scratch("sg_scr", [4, 128, 1024], BF16)
    dbg_scr = scratch("dbg_scr", [128, 1024], F32)
    ab_scr = scratch("ab_scr", [8, 128, 512], BF16)
    py_scr = scratch("py_scr", [128, 258], F32)
    qt_scr = scratch("qt_scr", [128, 512], BF16)

    with ExitStack() as outer:
        oe = outer.enter_context

        pe, act, dve, pool = TL("pe"), TL("act"), TL("dve"), TL("pool")
        qs, qg = TL("qs"), TL("qg")
        alltl = [pe, act, dve, pool, qs, qg]
        for tl in alltl:
            tl.sem = oe(nc.semaphore(tl.name))

        consts_sb = oe(nc.sbuf_tensor("consts_sb", [128, len(CONST_COLS)], F32))
        dyn_sb = oe(nc.sbuf_tensor("dyn_sb", [1, 96], I32))
        tri_sb = oe(nc.sbuf_tensor("tri_sb", [128, 128], BF16))
        ublk_sb = oe(nc.sbuf_tensor("ublk_sb", [128, 128], F32))
        ones32_sb = oe(nc.sbuf_tensor("ones32_sb", [32, 128], F32))
        ident_sb = oe(nc.sbuf_tensor("ident_sb", [128, 128], F32))
        ones1f = oe(nc.sbuf_tensor("ones1f", [1, 128], F32))
        gpt = oe(nc.sbuf_tensor("gpt", [128, 256], F32))
        logg_pt = oe(nc.sbuf_tensor("logg_pt", [128, 256], F32))
        c_virt = oe(nc.sbuf_tensor("c_virt", [128, VBLK * 8], F32))
        eps_own = oe(nc.sbuf_tensor("eps_own", [128, 16 * 8], F32))
        bsum = oe(nc.sbuf_tensor("bsum", [32, 8], F32))

        def cc(name):
            i = CONST_COLS.index(name)
            return consts_sb[:, i:i + 1]

        for k, v in CONSTS.items():
            nc.const_aps.register(F32, float(v), cc(k))

        def dyn_gather(dst_ap, src_dram, j, width, max_val, rows=False):
            def f(e):
                with e.register(_rn()) as r:
                    e.reg_load(r, dyn_sb[0:1, j:j + 1])
                    off = e.snap(r, donate=True, min_val=0, max_val=max_val)
                    if rows:
                        s = src_dram[ds(off, width), :]
                    else:
                        s = src_dram[:, ds(off, width)]
                    return e.dma_start(dst_ap, s)
            return f

        def barrier(emm):
            snap = [(tl, tl.n) for tl in alltl]
            for eng in Emitter.ENGS:
                for tl, n in snap:
                    emm.wait(eng, tl, n)

        em = Emitter()
        for sb, dr in ((consts_sb, consts), (dyn_sb, dyn), (tri_sb, tri),
                       (ublk_sb, ublk), (ones32_sb, ones32), (ident_sb, ident)):
            em.do("sync", lambda e, sb=sb, dr=dr: e.dma_start(sb[:], dr[:, :]),
                  inc=qs, inc_k=16)
        em.do("vector", lambda e: e.memset(ones1f[:], 1.0), inc=dve)
        em.replay(nc)

        if max_phase < 1:
            return nc
        with ExitStack() as ph:
            pp = ph.enter_context
            em = Emitter()
            barrier(em)
            NR = 3
            x_b = [pp(nc.sbuf_tensor(f"a_x{i}", [128, C], F32)) for i in range(NR)]
            xq_b = [pp(nc.sbuf_tensor(f"a_xq{i}", [128, C], F32)) for i in range(2)]
            xn_b = [pp(nc.sbuf_tensor(f"a_xn{i}", [128, C], BF16)) for i in range(NR)]
            st_b = [pp(nc.sbuf_tensor(f"a_st{i}", [128, 8], F32)) for i in range(NR)]

            tiles = [(x_all, xn_scr, i) for i in range(32)] + \
                    [(x_own, xn_own_scr, i) for i in range(16)]
            x_free = [0] * NR
            x_free_a = [0] * NR
            st_free = [0] * NR
            xq_free = [0] * 2
            xn_free = [0] * NR
            for ti, (src, dst, i) in enumerate(tiles):
                bi, qi = ti % NR, ti % 2
                xb, xnb, stb, xqb = x_b[bi], xn_b[bi], st_b[bi], xq_b[qi]
                em.wait("sync", dve, x_free[bi])
                em.wait("sync", act, x_free_a[bi])
                ld = em.do("sync", lambda e, xb=xb, src=src, i=i: e.dma_start(
                    xb[:], src[i * 128:(i + 1) * 128, :]), inc=qs, inc_k=16)
                em.wait("vector", qs, ld)
                em.wait("vector", act, st_free[bi])
                em.do("vector", lambda e, xb=xb, stb=stb: e.reduce_sum(
                    stb[:, 0:1], xb[:], axis=AX.X), inc=dve)
                em.wait("scalar", qs, ld)
                em.wait("scalar", act, xq_free[qi])
                sq = em.do("scalar", lambda e, xb=xb, xqb=xqb: e.activation(
                    xqb[:], xb[:], AF.Square), inc=act)
                x_free_a[bi] = act.n
                em.wait("vector", act, sq)
                s2n = em.do("vector", lambda e, xqb=xqb, stb=stb: e.reduce_sum(
                    stb[:, 1:2], xqb[:], axis=AX.X), inc=dve)
                em.wait("scalar", dve, s2n)
                ng = em.do("scalar", lambda e, stb=stb: e.mul(
                    stb[:, 2:3], stb[:, 0:1], -1.0 / 2048.0), inc=act)
                em.wait("vector", act, ng)
                vv = em.do("vector", lambda e, stb=stb: e.scalar_tensor_tensor(
                    stb[:, 4:5], stb[:, 0:1], stb[:, 2:3], stb[:, 1:2],
                    ALU.mult, ALU.add), inc=dve)
                em.wait("scalar", dve, vv)
                sd = em.do("scalar", lambda e, stb=stb: e.activation(
                    stb[:, 5:6], stb[:, 4:5], AF.Sqrt, scale=1.0 / 2048.0,
                    bias=cc("ln_eps")), inc=act)
                em.wait("vector", act, sd)
                rr = em.do("vector", lambda e, stb=stb: e.reciprocal(
                    stb[:, 6:7], stb[:, 5:6]), inc=dve)
                xc = em.do("vector", lambda e, xb=xb, xqb=xqb, stb=stb:
                    e.tensor_scalar(xqb[:], xb[:], stb[:, 2:3], None, ALU.add),
                    inc=dve)
                x_free[bi] = dve.n
                em.wait("scalar", dve, xc)
                em.wait("scalar", qs, xn_free[bi])
                xn = em.do("scalar", lambda e, xqb=xqb, xnb=xnb, stb=stb:
                    e.activation(xnb[:], xqb[:], AF.Copy, scale=stb[:, 6:7]),
                    inc=act)
                xq_free[qi] = act.n
                st_free[bi] = act.n
                em.wait("sync", act, xn)
                xn_free[bi] = em.do("sync", lambda e, xnb=xnb, dst=dst, i=i:
                    e.dma_start(dst[i * 128:(i + 1) * 128, :], xnb[:]),
                    inc=qs, inc_k=16)
            em.replay(nc)

        if max_phase < 2:
            return nc
        with ExitStack() as ph:
            pp = ph.enter_context
            em = Emitter()
            barrier(em)
            wq_sb = [pp(nc.sbuf_tensor(f"wq{k}", [128, QD], BF16)) for k in range(16)]
            wk_sb = [pp(nc.sbuf_tensor(f"wk{k}", [128, KD], BF16)) for k in range(16)]
            wv_sb = [pp(nc.sbuf_tensor(f"wv{k}", [128, KD], BF16)) for k in range(16)]
            wg_sb = [pp(nc.sbuf_tensor(f"wg{k}", [128, H], BF16)) for k in range(16)]
            for k in range(16):
                for sb, dr in ((wq_sb, wq), (wk_sb, wk), (wv_sb, wv), (wg_sb, wg)):
                    em.do("gpsimd", lambda e, sb=sb, dr=dr, k=k: e.dma_start(
                        sb[k][:], dr[k * 128:(k + 1) * 128, :]), inc=qg, inc_k=16)
            w_ld = qg.n

            xnf = [[pp(nc.sbuf_tensor(f"xnf{s}_{k}", [128, 512], BF16))
                    for k in range(16)] for s in range(2)]
            kv_st = [pp(nc.sbuf_tensor(f"kvst{i}", [128, 512], BF16)) for i in range(4)]
            v_st = [pp(nc.sbuf_tensor(f"vst{i}", [128, 1032], BF16)) for i in range(4)]
            g_st = [pp(nc.sbuf_tensor(f"gst{i}", [8, 512], F32)) for i in range(2)]
            zt = pp(nc.sbuf_tensor("zt", [128, 256], F32))
            azt = pp(nc.sbuf_tensor("azt", [128, 256], F32))
            zt2 = pp(nc.sbuf_tensor("zt2", [128, 256], F32))

            ps_a = [pp(nc.psum_tensor(f"ps_a{i}", [128, 512], F32)) for i in range(2)]
            ps_v = [pp(nc.psum_tensor(f"ps_v{i}", [128, 1024], F32)) for i in range(2)]
            ps_g = pp(nc.psum_tensor("ps_g", [8, 512], F32))
            ps_t = pp(nc.psum_tensor("ps_t", [128, 8], F32))

            em.do("vector", lambda e: e.memset(kv_st[0][:], 0.0), inc=dve)
            em.do("vector", lambda e: e.memset(v_st[0][:], 0.0), inc=dve)
            zf = dve.n
            em.wait("sync", dve, zf)
            for h in range(8):
                em.do("sync", lambda e, h=h: e.dma_start(
                    k_scr[h * 128:(h + 1) * 128, 0:512], kv_st[0][:]),
                    inc=qs, inc_k=16)
            for r4 in range(4):
                em.do("sync", lambda e, r4=r4: e.dma_start(
                    v_scr[r4 * 128:(r4 + 1) * 128, :], v_st[0][:]),
                    inc=qs, inc_k=16)
            zf_st = qs.n
            for i in range(4):
                em.wait("vector", qs, zf_st)
                em.do("vector", lambda e, i=i: e.memset(
                    v_st[i][:, :].rearrange("p (h x) -> p h x", x=129)[:, :, 128:129],
                    1.0), inc=dve)
            ones_set = dve.n
            em.wait("tensor", qg, w_ld)

            xnf_ready = [0, 0]
            xnf_free = [0, 0]
            kv_free = [zf_st, 0, 0, 0]
            v_free = [zf_st, 0, 0, 0]
            g_free = [0, 0]
            ps_a_free = [0, 0]
            ps_v_free = [0, 0]

            def load_xnf(sb, S, scr, w):
                for kc in range(16):
                    em.do("sync", lambda e, sb=sb, kc=kc, S=S, scr=scr, w=w:
                          e.dma_start_transpose(
                              sb[kc][:, 0:w],
                              scr[S * w:(S + 1) * w, kc * 128:(kc + 1) * 128]),
                          inc=qs, inc_k=16)
                return qs.n

            def proj_block(xt, n_m, w_list, col, stage_mod, dst_fn):
                pa = proj_block.cnt % 2
                proj_block.cnt += 1
                em.wait("tensor", act, ps_a_free[pa])
                for k in range(16):
                    kwargs = dict(start=(k == 0), stop=(k == 15))
                    mmf = (lambda e, k=k, pa=pa, xt=xt, w=w_list, col=col:
                           e.matmul(ps_a[pa][:], w[k][:, col:col + 128], xt[k][:],
                                    **{"start": k == 0, "stop": k == 15}))
                    if k == 15:
                        em.do("tensor", mmf, inc=pe)
                    else:
                        em.do("tensor", mmf)
                mmdone = pe.n
                st = stage_mod % 4
                em.wait("scalar", qs, kv_free[st])
                em.wait("scalar", pe, mmdone)
                kvr = em.do("scalar", lambda e, pa=pa, st=st: e.activation(
                    kv_st[st][:], ps_a[pa][:], AF.Copy), inc=act)
                ps_a_free[pa] = act.n
                em.wait("sync", act, kvr)
                kv_free[st] = em.do("sync", dst_fn(kv_st[st]), inc=qs, inc_k=16)

            proj_block.cnt = 0

            for S in range(8):
                sb2 = S % 2
                em.wait("sync", pe, xnf_free[sb2])
                xnf_ready[sb2] = load_xnf(xnf[sb2], S, xn_scr, 512)
                em.wait("tensor", qs, xnf_ready[sb2])
                xt = xnf[sb2]
                for h in range(8):
                    proj_block(xt, 128, wk_sb, h * 128, h,
                               lambda stg, h=h, S=S: (lambda e: e.dma_start(
                                   k_scr[h * 128:(h + 1) * 128,
                                         512 + S * 512:512 + (S + 1) * 512],
                                   stg[:])))
                for tc in range(4):
                    pv = tc % 2
                    em.wait("tensor", act, ps_v_free[pv])
                    for k in range(16):
                        for hn in range(2):
                            mmf = (lambda e, k=k, tc=tc, pv=pv, hn=hn, xt=xt:
                                   e.matmul(ps_v[pv][:, hn * 512:(hn + 1) * 512],
                                            xt[k][:, tc * 128:(tc + 1) * 128],
                                            wv_sb[k][:, hn * 512:(hn + 1) * 512],
                                            start=(k == 0), stop=(k == 15)))
                            if k == 15 and hn == 1:
                                em.do("tensor", mmf, inc=pe)
                            else:
                                em.do("tensor", mmf)
                    mmv = pe.n
                    vs = tc % 4
                    em.wait("scalar", dve, ones_set)
                    em.wait("scalar", qs, v_free[vs])
                    em.wait("scalar", pe, mmv)
                    vr = em.do("scalar", lambda e, pv=pv, vs=vs: e.activation(
                        v_st[vs][:, :].rearrange("p (h x) -> p h x", x=129)[:, :, 0:128],
                        ps_v[pv][:, :].rearrange("p (h x) -> p h x", x=128),
                        AF.Copy), inc=act)
                    ps_v_free[pv] = act.n
                    em.wait("sync", act, vr)
                    v_free[vs] = em.do("sync", lambda e, vs=vs, S=S, tc=tc: e.dma_start(
                        v_scr[512 + S * 512 + tc * 128:512 + S * 512 + (tc + 1) * 128,
                              :], v_st[vs][:]), inc=qs, inc_k=16)
                gb = S % 2
                em.wait("tensor", dve, g_free[gb])
                for k in range(16):
                    mmf = (lambda e, k=k, xt=xt: e.matmul(
                        ps_g[:], wg_sb[k][:], xt[k][:],
                        start=(k == 0), stop=(k == 15)))
                    em.do("tensor", mmf, inc=pe if k == 15 else None)
                mmg = pe.n
                em.wait("vector", pe, mmg)
                gv = em.do("vector", lambda e, gb=gb: e.tensor_copy(
                    g_st[gb][:], ps_g[:]), inc=dve)
                for tc in range(4):
                    em.wait("tensor", dve, gv)
                    em.wait("tensor", dve, dve.n)
                    tr = em.do("tensor", lambda e, gb=gb, tc=tc: e.transpose(
                        ps_t[:], g_st[gb][:, tc * 128:(tc + 1) * 128],
                        ident_sb[0:8, 0:8]), inc=pe)
                    em.wait("vector", pe, tr)
                    em.do("vector", lambda e, S=S, tc=tc: e.tensor_copy(
                        gpt[:, (S * 4 + tc) * 8:(S * 4 + tc + 1) * 8], ps_t[:]),
                        inc=dve)
                g_free[gb] = dve.n
                xnf_free[sb2] = pe.n

            for So in range(4):
                sb2 = So % 2
                em.wait("sync", pe, xnf_free[sb2])
                xnf_ready[sb2] = load_xnf(xnf[sb2], So, xn_own_scr, 512)
                em.wait("tensor", qs, xnf_ready[sb2])
                xt = xnf[sb2]
                for qd in range(16):
                    proj_block(xt, 128, wq_sb, qd * 128, qd,
                               lambda stg, qd=qd, So=So: (lambda e: e.dma_start(
                                   q_scr[qd * 128:(qd + 1) * 128,
                                         So * 512:(So + 1) * 512], stg[:])))
                xnf_free[sb2] = pe.n

            glast = dve.n
            em.wait("vector", dve, glast)
            zz = em.do("vector", lambda e: e.tensor_scalar(
                zt[:], gpt[:], GATE_OFFSET, None, ALU.add), inc=dve)
            em.wait("scalar", dve, zz)
            az = em.do("scalar", lambda e: e.activation(
                azt[:], zt[:], AF.Abs), inc=act)
            em.wait("vector", act, az)
            ne = em.do("vector", lambda e: e.tensor_scalar(
                zt2[:], azt[:], -1.0, None, ALU.mult), inc=dve)
            em.wait("scalar", dve, ne)
            en = em.do("scalar", lambda e: e.activation(
                azt[:], zt2[:], AF.Exp), inc=act)
            em.wait("vector", act, en)
            en1 = em.do("vector", lambda e: e.tensor_scalar(
                azt[:], azt[:], 1.0, None, ALU.add), inc=dve)
            em.wait("scalar", dve, en1)
            l1 = em.do("scalar", lambda e: e.activation(
                azt[:], azt[:], AF.Ln), inc=act)
            em.wait("vector", act, l1)
            em.do("vector", lambda e: e.scalar_tensor_tensor(
                gpt[:], zt[:], 0.0, azt[:], ALU.min, ALU.subtract), inc=dve)
            lgr = dve.n
            em.wait("tensor", dve, lgr)
            for blk in range(32):
                em.do("tensor", lambda e, blk=blk: e.matmul(
                    ps_t[:], ublk_sb[:], gpt[:, blk * 8:(blk + 1) * 8],
                    start=True, stop=True), inc=pe)
                em.wait("vector", pe, pe.n)
                em.do("vector", lambda e, blk=blk: e.tensor_copy(
                    logg_pt[:, blk * 8:(blk + 1) * 8], ps_t[:]), inc=dve)
                em.wait("tensor", dve, dve.n)
            cum_done = dve.n
            em.wait("sync", dve, cum_done)
            r127 = em.do("sync", lambda e: e.dma_start(
                tmp_scr[:, :].rearrange("(o b) h -> o (b h)", o=1),
                logg_pt[127:128, :]), inc=qs, inc_k=16)
            em.wait("sync", qs, r127)
            bs = em.do("sync", lambda e: e.dma_start(
                bsum[:, :], tmp_scr[:, :]), inc=qs, inc_k=16)
            em.wait("tensor", qs, bs)
            for blk in range(1, 32):
                em.do("tensor", lambda e, blk=blk: e.matmul(
                    ps_t[:], ones32_sb[0:blk, :], bsum[0:blk, :],
                    start=True, stop=True), inc=pe)
                em.wait("vector", pe, pe.n)
                em.do("vector", lambda e, blk=blk: e.tensor_tensor(
                    logg_pt[:, blk * 8:(blk + 1) * 8],
                    logg_pt[:, blk * 8:(blk + 1) * 8], ps_t[:], ALU.add), inc=dve)
                em.wait("tensor", dve, dve.n)
            logg_done = dve.n
            if "logg_scr" in dbg:
                em.wait("sync", dve, logg_done)
                em.do("sync", lambda e: e.dma_start(logg_scr[:, :], logg_pt[:]),
                      inc=qs, inc_k=16)
            c_all, eps_all = zt, azt
            em.wait("scalar", dve, logg_done)
            em.do("scalar", lambda e: e.activation(
                c_all[:], logg_pt[:], AF.Exp, scale=-0.5, bias=cc("c_bias")), inc=act)
            ea = em.do("scalar", lambda e: e.activation(
                eps_all[:], logg_pt[:], AF.Exp, scale=-1.0, bias=cc("eps_bias")),
                inc=act)
            zp = em.do("vector", lambda e: e.memset(c_virt[:, 0:64], 0.0), inc=dve)
            em.wait("sync", dve, zp)
            em.wait("sync", act, ea)
            em.do("sync", lambda e: e.dma_start(cg_scr[:, 0:32], c_virt[:, 0:32]),
                  inc=qs, inc_k=16)
            em.do("sync", lambda e: e.dma_start(cg_scr[:, 288:320], c_virt[:, 32:64]),
                  inc=qs, inc_k=16)
            em.do("sync", lambda e: e.dma_start(cg_scr[:, 32:288], c_all[:]),
                  inc=qs, inc_k=16)
            st_t = em.do("sync", lambda e: e.dma_start(eps_scr[:, :], eps_all[:]),
                         inc=qs, inc_k=16)
            em.wait("sync", qs, st_t)
            em.do("sync", dyn_gather(c_virt[:, :], cg_scr, J_CV, VBLK * 8, 32),
                  inc=qs, inc_k=16)
            for ob in range(4):
                em.do("sync", dyn_gather(
                    eps_own[:, ob * 32:(ob + 1) * 32],
                    eps_scr[:, 64 * ob:64 * ob + 64], J_EPS, 32, 32),
                    inc=qs, inc_k=16)
            em.replay(nc)

        if max_phase < 3:
            return nc
        with ExitStack() as ph:
            pp = ph.enter_context
            em = Emitter()
            barrier(em)
            xnfb = [pp(nc.sbuf_tensor(f"p1xn{k}", [128, 1024], BF16)) for k in range(16)]
            hdn = [pp(nc.sbuf_tensor(f"hdn{i}", [128, 1024], BF16)) for i in range(NI)]
            gw_b = [pp(nc.sbuf_tensor(f"gw{i}", [128, 2048], BF16)) for i in range(2)]
            uw_b = [pp(nc.sbuf_tensor(f"uw{i}", [128, 2048], BF16)) for i in range(2)]
            sg_b = [pp(nc.sbuf_tensor(f"sg{i}", [128, 1024], BF16)) for i in range(2)]
            uu_b = [pp(nc.sbuf_tensor(f"uu{i}", [128, 1024], BF16)) for i in range(2)]
            dw_ch = [pp(nc.sbuf_tensor(f"dwch{i}", [128, 11 * 512], BF16))
                     for i in range(2)]
            gst_sb = pp(nc.sbuf_tensor("gst_sb", [128, NI], F32))
            ust_sb = pp(nc.sbuf_tensor("ust_sb", [128, NI], F32))
            dsb_row = pp(nc.sbuf_tensor("dsb_row", [1, C], F32))
            ds_bc = pp(nc.sbuf_tensor("ds_bc", [128, C], F32))
            t1_b = [pp(nc.sbuf_tensor(f"t1_{i}", [128, 512], F32)) for i in range(2)]
            go_b = [pp(nc.sbuf_tensor(f"go_{i}", [128, 512], F32)) for i in range(2)]

            ps_gu = [pp(nc.psum_tensor(f"ps_gu{i}", [128, 1024], F32)) for i in range(4)]

            em.do("sync", lambda e: e.dma_start(gst_sb[:], gate_st[:, :]),
                  inc=qs, inc_k=16)
            em.do("sync", lambda e: e.dma_start(ust_sb[:], up_st[:, :]),
                  inc=qs, inc_k=16)
            em.do("sync", lambda e: e.dma_start(dsb_row[:], down_s[:, :]),
                  inc=qs, inc_k=16)
            sc_ld = qs.n
            em.wait("tensor", qs, sc_ld)
            for rnd in range(2):
                for hn in range(4):
                    em.do("tensor", lambda e, hn=hn, rnd=rnd: e.matmul(
                        ps_gu[0][:, hn * 256:(hn + 1) * 256], ones1f[:, :],
                        dsb_row[:, rnd * 1024 + hn * 256:rnd * 1024 + (hn + 1) * 256],
                        start=True, stop=True), inc=pe)
                em.wait("vector", pe, pe.n)
                em.do("vector", lambda e, rnd=rnd: e.tensor_copy(
                    ds_bc[:, rnd * 1024:(rnd + 1) * 1024], ps_gu[0][:]), inc=dve)
                em.wait("tensor", dve, dve.n)
            dsbc_done = dve.n

            for S in range(2 if P1_SUB >= 1 else 0):
                em.wait("sync", pe, pe.n)
                for kc in range(16):
                    em.do("sync", lambda e, kc=kc, S=S: e.dma_start_transpose(
                        xnfb[kc][:], xn_own_scr[S * 1024:(S + 1) * 1024,
                                                kc * 128:(kc + 1) * 128]),
                        inc=qs, inc_k=16)
                xn_ld = qs.n
                em.wait("tensor", qs, xn_ld)
                em.wait("tensor", dve, dsbc_done)
                em.wait("tensor", act, act.n)

                gw_free = [0] * 2
                hdn_done = 0
                for i in range(NI if P1_GRAN >= 2 else 0):
                    bw = i % 2
                    em.wait("gpsimd", pe, gw_free[bw])
                    em.do("gpsimd", lambda e, bw=bw, i=i: e.dma_start(
                        gw_b[bw][:, :].rearrange("p (k c) -> p k c", c=128),
                        gate_wt[:, i * 128:(i + 1) * 128].rearrange(
                            "(k p) c -> p k c", p=128)), inc=qg, inc_k=16)
                    em.do("gpsimd", lambda e, bw=bw, i=i: e.dma_start(
                        uw_b[bw][:, :].rearrange("p (k c) -> p k c", c=128),
                        up_wt[:, i * 128:(i + 1) * 128].rearrange(
                            "(k p) c -> p k c", p=128)), inc=qg, inc_k=16)
                    w_rdy = qg.n
                    pg, pu = ps_gu[(i % 2) * 2], ps_gu[(i % 2) * 2 + 1]
                    if P1_GRAN < 3:
                        continue
                    em.wait("tensor", qg, w_rdy)
                    em.wait("tensor", act, getattr(em, f"_p1sg{i%2}", 0) or 0)
                    em.wait("tensor", dve, getattr(em, f"_p1uu{i%2}", 0) or 0)
                    for k in range(16):
                        for hn in range(2):
                            mmf = (lambda e, k=k, hn=hn, bw=bw, pg=pg: e.matmul(
                                pg[:, hn * 512:(hn + 1) * 512],
                                gw_b[bw][:, k * 128:(k + 1) * 128],
                                xnfb[k][:, hn * 512:(hn + 1) * 512],
                                start=(k == 0), stop=(k == 15)))
                            em.do("tensor", mmf,
                                  inc=pe if (k == 15 and hn == 1) else None)
                    g_mm = pe.n
                    for k in range(16):
                        for hn in range(2):
                            mmf = (lambda e, k=k, hn=hn, bw=bw, pu=pu: e.matmul(
                                pu[:, hn * 512:(hn + 1) * 512],
                                uw_b[bw][:, k * 128:(k + 1) * 128],
                                xnfb[k][:, hn * 512:(hn + 1) * 512],
                                start=(k == 0), stop=(k == 15)))
                            em.do("tensor", mmf,
                                  inc=pe if (k == 15 and hn == 1) else None)
                    u_mm = pe.n
                    gw_free[bw] = pe.n
                    if P1_GRAN < 4:
                        continue
                    sgb, uub = sg_b[i % 2], uu_b[i % 2]
                    em.wait("scalar", pe, g_mm)
                    em.wait("scalar", dve, hdn_done)
                    sg_done = em.do("scalar", lambda e, sgb=sgb, pg=pg, i=i:
                        e.activation(sgb[:], pg[:], AF.Silu,
                                     scale=gst_sb[:, i:i + 1]), inc=act)
                    setattr(em, f"_p1sg{i%2}", act.n)
                    if "dbg_scr" in dbg and S == 0 and i == 0:
                        em.wait("vector", pe, g_mm)
                        pr = em.do("vector", lambda e, pg=pg: e.tensor_copy(
                            t1_b[0][:], pg[:, 0:512]), inc=dve)
                        em.wait("sync", dve, pr)
                        em.do("sync", lambda e: e.dma_start(
                            dbg_scr[:, 0:512], t1_b[0][:]), inc=qs, inc_k=16)
                        em.do("sync", lambda e: e.dma_start(
                            dbg_scr[0:128, 512:512 + NI].rearrange("p c -> p c"),
                            gst_sb[:, :]), inc=qs, inc_k=16)
                    if P1_GRAN < 5:
                        continue
                    em.wait("vector", pe, u_mm)
                    em.do("vector", lambda e, uub=uub, pu=pu, i=i: e.tensor_scalar(
                        uub[:], pu[:], ust_sb[:, i:i + 1], None, ALU.mult), inc=dve)
                    setattr(em, f"_p1uu{i%2}", dve.n)
                    if P1_GRAN < 6:
                        continue
                    em.wait("vector", act, sg_done)
                    hdn_done = em.do("vector", lambda e, i=i, sgb=sgb, uub=uub:
                        e.tensor_mul(hdn[i][:], sgb[:], uub[:]), inc=dve)
                    if "hdn_scr" in dbg and S == 0:
                        em.wait("sync", dve, hdn_done)
                        em.do("sync", lambda e, i=i: e.dma_start(
                            hdn_scr[i, :, :], hdn[i][:]), inc=qs, inc_k=16)
                        if i < 2:
                            em.do("sync", lambda e, i=i, sgb=sgb: e.dma_start(
                                sg_scr[i, :, :], sgb[:]), inc=qs, inc_k=16)
                            em.do("sync", lambda e, i=i, uub=uub: e.dma_start(
                                sg_scr[2 + i, :, :], uub[:]), inc=qs, inc_k=16)

                hdn_all = dve.n
                em.wait("tensor", dve, hdn_all)
                em.wait("tensor", dve, dsbc_done)
                go_free = [0] * 2
                t1_free = [0] * 2
                dw_free = [0] * 2
                ps_d = ps_gu
                for cs in range(4 if P1_SUB >= 2 else 0):
                    for th in range(2):
                        for ic in range(4):
                            bw2 = ic % 2
                            em.wait("gpsimd", pe, dw_free[bw2])
                            dwl = em.do("gpsimd", lambda e, cs=cs, ic=ic, bw2=bw2:
                                e.dma_start(
                                    dw_ch[bw2][:, :].rearrange(
                                        "p (i c) -> p i c", c=512),
                                    down_wt[ic * 1408:(ic + 1) * 1408,
                                            cs * 512:(cs + 1) * 512].rearrange(
                                        "(i p) c -> p i c", p=128)),
                                inc=qg, inc_k=16)
                            em.wait("tensor", qg, dwl)
                            for il in range(11):
                                i = ic * 11 + il
                                for tc in range(4):
                                    pd = ps_d[tc // 2][:, (tc % 2) * 512:
                                                       (tc % 2 + 1) * 512]
                                    mmf = (lambda e, i=i, il=il, tc=tc, th=th,
                                           pd=pd, bw2=bw2: e.matmul(
                                        pd, hdn[i][:, (th * 4 + tc) * 128:
                                                   (th * 4 + tc + 1) * 128],
                                        dw_ch[bw2][:, il * 512:(il + 1) * 512],
                                        start=(i == 0), stop=(i == NI - 1)))
                                    em.do("tensor", mmf,
                                          inc=pe if (il == 10 and tc == 3) else None)
                            dw_free[bw2] = pe.n
                        d_mm = pe.n
                        for tc in range(4):
                            pd = ps_d[tc // 2][:, (tc % 2) * 512:(tc % 2 + 1) * 512]
                            t1 = t1_b[tc % 2]
                            gob = go_b[tc % 2]
                            em.wait("vector", pe, d_mm)
                            em.wait("vector", act, t1_free[tc % 2])
                            d1 = em.do("vector", lambda e, pd=pd, t1=t1, cs=cs:
                                e.tensor_tensor(t1[:], pd,
                                                ds_bc[:, cs * 512:(cs + 1) * 512],
                                                ALU.mult), inc=dve)
                            em.wait("scalar", dve, d1)
                            em.wait("scalar", qs, go_free[tc % 2])
                            gs = em.do("scalar", lambda e, t1=t1, gob=gob:
                                e.activation(gob[:], t1[:], AF.Silu), inc=act)
                            t1_free[tc % 2] = act.n
                            em.wait("sync", act, gs)
                            go_free[tc % 2] = em.do("sync",
                                lambda e, gob=gob, S=S, th=th, tc=tc, cs=cs:
                                e.dma_start(
                                    gsig_scr[S * 1024 + th * 512 + tc * 128:
                                             S * 1024 + th * 512 + (tc + 1) * 128,
                                             cs * 512:(cs + 1) * 512], gob[:]),
                                inc=qs, inc_k=16)
                        em.wait("tensor", dve, dve.n)
            em.replay(nc)

        if max_phase < 4:
            return nc
        with ExitStack() as ph:
            pp = ph.enter_context
            em = Emitter()
            barrier(em)
            k_res = pp(nc.sbuf_tensor("kres", [128, 8 * VBLK * 128], BF16))
            v_res = pp(nc.sbuf_tensor("vres", [128, VBLK * 1032], BF16))
            q_b = [pp(nc.sbuf_tensor(f"qb{i}", [128, 512], BF16)) for i in range(2)]
            ab = [pp(nc.sbuf_tensor(f"ab{i}", [128, 512], BF16)) for i in range(4)]
            tmp_bf = pp(nc.sbuf_tensor("tmp_bf", [128, 512], BF16))
            yn_b = [pp(nc.sbuf_tensor(f"ynb{i}", [128, 128], BF16)) for i in range(4)]
            ep_b = pp(nc.sbuf_tensor("ep_b", [128, 8], F32))

            py_dbg = pp(nc.sbuf_tensor("py_dbg", [128, 258], F32))
            ps_qk = [pp(nc.psum_tensor(f"ps_qk{i}", [128, 512], F32)) for i in range(2)]
            py = [pp(nc.psum_tensor(f"py{i}", [128, 129], F32)) for i in range(4)]

            def kload(e):
                with e.register(_rn()) as r:
                    e.reg_load(r, dyn_sb[0:1, J_KOFF:J_KOFF + 1])
                    off = e.snap(r, donate=True, min_val=0, max_val=512)
                    src = k_scr[:, ds(off, VBLK * 128)].rearrange(
                        "(h p) c -> p h c", p=128)
                    return e.dma_start(
                        k_res[:, :].rearrange("p (h c) -> p h c", c=VBLK * 128),
                        src)

            def vload(e):
                with e.register(_rn()) as r:
                    e.reg_load(r, dyn_sb[0:1, J_VOFF:J_VOFF + 1])
                    off = e.snap(r, donate=True, min_val=0, max_val=512)
                    src = v_scr[ds(off, VBLK * 128), :].rearrange(
                        "(b p) x -> p b x", p=128)
                    return e.dma_start(
                        v_res[:, :].rearrange("p (b x) -> p b x", x=1032),
                        src)

            em.do("sync", kload, inc=qs, inc_k=16)
            em.do("sync", vload, inc=qs, inc_k=16)
            kv_ld = qs.n
            em.wait("tensor", qs, kv_ld)

            q_ready = [0, 0]
            q_free = [0, 0]

            def q_load(idx, os, h):
                em.wait("sync", pe, q_free[idx])
                q_ready[idx] = em.do("sync", lambda e, os=os, h=h, idx=idx:
                    e.dma_start(q_b[idx][:],
                                q_scr[h * 128:(h + 1) * 128,
                                      os * 512:(os + 1) * 512]), inc=qs, inc_k=16)

            jobs = [(os, h) for os in range(4) for h in range(16)]
            q_load(0, *jobs[0])
            ep_act, ep_dve = 0, 0
            bank_evac = [None, None]
            abuf_mm2 = [0] * 4

            for ji, (os, h) in enumerate(jobs):
                kbmax = 8 * os + 8
                hv = h // 2
                qi = ji % 2
                if ji + 1 < len(jobs):
                    q_load(1 - qi, *jobs[ji + 1])
                em.wait("tensor", qs, q_ready[qi])
                evac_done = {}

                def emit_mm2(kb, kbmax=None, hv=None, evac_done=None):
                    j = kb - (kbmax - 4)
                    tcs = range(max(j, 0), 4)
                    etl, ecnt = evac_done[kb]
                    em.wait("tensor", etl, ecnt)
                    last_tc = max(tcs)
                    for tc in tcs:
                        stop = (kb == kbmax - 4 + tc)
                        pyt = py[tc][:, 0:129]
                        mmf = (lambda e, kb=kb, tc=tc, pyt=pyt, stop=stop, hv=hv:
                               e.matmul(pyt, ab[kb % 4][:, tc * 128:(tc + 1) * 128],
                                        v_res[:, kb * 1032 + hv * 129:
                                              kb * 1032 + hv * 129 + 129],
                                        start=(kb == 0), stop=stop))
                        em.do("tensor", mmf, inc=pe if tc == last_tc else None)
                    abuf_mm2[kb % 4] = pe.n

                for kb in range(kbmax):
                    j = kb - (kbmax - 4)
                    pq = kb % 2
                    if bank_evac[pq] is not None:
                        etl, ecnt = bank_evac[pq]
                        em.wait("tensor", etl, ecnt)
                    if kb == 0:
                        em.wait("tensor", act, ep_act)
                        em.wait("tensor", dve, ep_dve)
                    n0 = j * 128 if j > 0 else 0
                    kcol = hv * VBLK * 128 + kb * 128
                    mmf = (lambda e, kb=kb, pq=pq, n0=n0, kcol=kcol, qi=qi: e.matmul(
                        ps_qk[pq][:, n0:512], k_res[:, kcol:kcol + 128],
                        q_b[qi][:, n0:512], start=True, stop=True))
                    em.do("tensor", mmf, inc=pe)
                    mm1_cnt = pe.n
                    a_war = abuf_mm2[kb % 4]
                    ccol = c_virt[:, kb * 8 + hv:kb * 8 + hv + 1]
                    if kb % 2 == 0 or P2_EVAC == "act":
                        em.wait("scalar", pe, mm1_cnt)
                        if a_war:
                            em.wait("scalar", pe, a_war)
                        ev = em.do("scalar", lambda e, kb=kb, pq=pq, ccol=ccol:
                            e.activation(ab[kb % 4][:], ps_qk[pq][:], AF.Square,
                                         scale=ccol), inc=act)
                        evac_done[kb] = (act, ev)
                        bank_evac[pq] = (act, ev)
                    else:
                        em.wait("vector", pe, mm1_cnt)
                        if a_war:
                            em.wait("vector", pe, a_war)
                        ts_cnt = em.do("vector", lambda e, pq=pq, ccol=ccol:
                            e.tensor_scalar(
                            tmp_bf[:], ps_qk[pq][:], ccol, None, ALU.mult), inc=dve)
                        bank_evac[pq] = (dve, ts_cnt)
                        ev = em.do("vector", lambda e, kb=kb: e.tensor_mul(
                            ab[kb % 4][:], tmp_bf[:], tmp_bf[:]), inc=dve)
                        evac_done[kb] = (dve, ev)
                    if j >= 0:
                        etl, ecnt = evac_done[kb]
                        em.wait("vector", etl, ecnt)
                        mk = em.do("vector", lambda e, kb=kb, j=j: e.tensor_mul(
                            ab[kb % 4][:, j * 128:(j + 1) * 128],
                            ab[kb % 4][:, j * 128:(j + 1) * 128], tri_sb[:]),
                            inc=dve)
                        evac_done[kb] = (dve, mk)
                    if kb >= 1:
                        emit_mm2(kb - 1, kbmax=kbmax, hv=hv, evac_done=evac_done)
                emit_mm2(kbmax - 1, kbmax=kbmax, hv=hv, evac_done=evac_done)
                q_free[qi] = pe.n

                all_mm2 = pe.n
                em.wait("vector", pe, all_mm2)
                em.wait("vector", act, ep_act)
                em.wait("scalar", pe, all_mm2)
                for tc in range(4):
                    oc = os * 4 + tc
                    pyt = py[tc]
                    base = 0
                    na = em.do("scalar", lambda e, pyt=pyt, base=base, tc=tc,
                               oc=oc, hv=hv:
                        e.add(ep_b[:, tc:tc + 1],
                              pyt[:, base + 128:base + 129],
                              eps_own[:, oc * 8 + hv:oc * 8 + hv + 1]), inc=act)
                    em.wait("vector", act, na)
                    rr = em.do("vector", lambda e, tc=tc: e.reciprocal(
                        ep_b[:, 4 + tc:5 + tc], ep_b[:, tc:tc + 1]), inc=dve)
                    em.wait("scalar", dve, rr)
                    em.wait("scalar", qs, getattr(em, f"_ynfree{tc}", 0) or 0)
                    yv = em.do("scalar", lambda e, pyt=pyt, base=base, tc=tc:
                        e.activation(yn_b[tc][:], pyt[:, base:base + 128], AF.Copy,
                                     scale=ep_b[:, 4 + tc:5 + tc]), inc=act)
                    em.wait("sync", act, yv)
                    st = em.do("sync", lambda e, tc=tc, os=os, h=h: e.dma_start(
                        y_scr[(os * 4 + tc) * 128:(os * 4 + tc + 1) * 128,
                              h * 128:(h + 1) * 128], yn_b[tc][:]), inc=qs, inc_k=16)
                    setattr(em, f"_ynfree{tc}", st)
                ep_act, ep_dve = act.n, dve.n
            em.replay(nc)

        if max_phase < 5:
            return nc
        with ExitStack() as ph:
            pp = ph.enter_context
            em = Emitter()
            barrier(em)
            wc_full = pp(nc.sbuf_tensor("wc_full", [128, 16 * 2048], BF16))
            yf = [pp(nc.sbuf_tensor(f"yf{q}", [128, 512], BF16)) for q in range(16)]
            bc_row = pp(nc.sbuf_tensor("bc_row", [1, C], F32))
            bc_bc = pp(nc.sbuf_tensor("bc_bc", [128, C], F32))
            xg_b = [pp(nc.sbuf_tensor(f"xg{i}", [128, 512], F32)) for i in range(4)]
            gg_b = [pp(nc.sbuf_tensor(f"gg{i}", [128, 512], F32)) for i in range(4)]
            o_b = [pp(nc.sbuf_tensor(f"ob{i}", [128, 512], F32)) for i in range(4)]
            t2_b = [pp(nc.sbuf_tensor(f"t2_{i}", [128, 512], F32)) for i in range(2)]
            ps_s = [pp(nc.psum_tensor(f"ps_s{i}", [128, 512], F32)) for i in range(4)]
            ps_b = pp(nc.psum_tensor("ps_b", [128, 512], F32))

            em.do("gpsimd", lambda e: e.dma_start(
                wc_full[:, :].rearrange("p (q c) -> p q c", c=2048),
                wc[:, :].rearrange("(q p) c -> p q c", p=128)),
                inc=qg, inc_k=16)
            em.do("sync", lambda e: e.dma_start(bc_row[:], bc[:, :]), inc=qs, inc_k=16)
            bc_ld = qs.n
            wc_ld = qg.n
            em.wait("tensor", qs, bc_ld)
            for hn in range(4):
                em.do("tensor", lambda e, hn=hn: e.matmul(
                    ps_b[:], ones1f[:, :], bc_row[:, hn * 512:(hn + 1) * 512],
                    start=True, stop=True), inc=pe)
                em.wait("vector", pe, pe.n)
                em.do("vector", lambda e, hn=hn: e.tensor_copy(
                    bc_bc[:, hn * 512:(hn + 1) * 512], ps_b[:]), inc=dve)
                em.wait("tensor", dve, dve.n)
            em.wait("tensor", qg, wc_ld)

            xg_free = [0] * 4
            o_free = [0] * 4
            for os in range(4):
                yld0 = qs.n
                for qd in range(16):
                    em.do("sync", lambda e, qd=qd, os=os: e.dma_start_transpose(
                        yf[qd][:], y_scr[os * 512:(os + 1) * 512,
                                         qd * 128:(qd + 1) * 128]),
                        inc=qs, inc_k=16)
                y_ld = qs.n
                em.wait("tensor", qs, y_ld)
                for cs in range(4):
                    xg_ld = [0] * 4
                    for tc in range(4):
                        em.wait("sync", dve, xg_free[tc])
                        em.do("sync", lambda e, tc=tc, os=os, cs=cs: e.dma_start(
                            xg_b[tc][:], x_own[os * 512 + tc * 128:
                                               os * 512 + (tc + 1) * 128,
                                               cs * 512:(cs + 1) * 512]),
                            inc=qs, inc_k=16)
                        xg_ld[tc] = em.do("sync",
                            lambda e, tc=tc, os=os, cs=cs: e.dma_start(
                                gg_b[tc][:], gsig_scr[os * 512 + tc * 128:
                                                      os * 512 + (tc + 1) * 128,
                                                      cs * 512:(cs + 1) * 512]),
                            inc=qs, inc_k=16)
                    for qd in range(16):
                        for tc in range(4):
                            mmf = (lambda e, qd=qd, tc=tc, cs=cs: e.matmul(
                                ps_s[tc][:], yf[qd][:, tc * 128:(tc + 1) * 128],
                                wc_full[:, qd * 2048 + cs * 512:
                                        qd * 2048 + (cs + 1) * 512],
                                start=(qd == 0), stop=(qd == 15)))
                            em.do("tensor", mmf,
                                  inc=pe if (qd == 15 and tc == 3) else None)
                    s_mm = pe.n
                    for tc in range(4):
                        t2 = t2_b[tc % 2]
                        em.wait("vector", pe, s_mm)
                        em.wait("vector", qs, xg_ld[tc])
                        em.do("vector", lambda e, tc=tc, t2=t2, cs=cs:
                            e.tensor_tensor(t2[:], ps_s[tc][:],
                                            bc_bc[:, cs * 512:(cs + 1) * 512],
                                            ALU.add), inc=dve)
                        em.do("vector", lambda e, tc=tc, t2=t2: e.tensor_mul(
                            t2[:], t2[:], gg_b[tc][:]), inc=dve)
                        em.wait("vector", qs, o_free[tc])
                        ov = em.do("vector", lambda e, tc=tc, t2=t2: e.tensor_add(
                            o_b[tc][:], t2[:], xg_b[tc][:]), inc=dve)
                        xg_free[tc] = dve.n
                        em.wait("sync", dve, ov)
                        o_free[tc] = em.do("sync",
                            lambda e, tc=tc, os=os, cs=cs: e.dma_start(
                                out_own[os * 512 + tc * 128:
                                        os * 512 + (tc + 1) * 128,
                                        cs * 512:(cs + 1) * 512], o_b[tc][:]),
                            inc=qs, inc_k=16)
                    em.wait("tensor", dve, dve.n)
            em.wait("sync", qs, qs.n)
            em.replay(nc)

    return nc


_BF = ml_dtypes.bfloat16


def _host_inputs(core, full):
    b, half = core // 2, core % 2
    x = full["L0_attn_hidden_states"]
    xb = np.ascontiguousarray(x[b]).astype(np.float32, copy=False)
    own = np.concatenate([xb[1024 * os + 512 * half:1024 * os + 512 * half + 512]
                          for os in range(4)], axis=0)
    shiftb = 4 * (1 - half)
    dyn = np.zeros((1, 96), dtype=np.int32)
    dyn[0, J_KOFF] = 512 - shiftb * 128
    dyn[0, J_VOFF] = 512 - shiftb * 128
    dyn[0, J_CV] = 32 - shiftb * 8
    dyn[0, J_EPS] = 32 * half
    consts_arr = np.zeros((128, len(CONST_COLS)), dtype=np.float32)
    for i, k in enumerate(CONST_COLS):
        consts_arr[:, i] = CONSTS[k]
    d = {
        "x_all": xb,
        "x_own": np.ascontiguousarray(own),
        "wq": full["L0_attn_Wq"], "wk": full["L0_attn_Wk"],
        "wv": full["L0_attn_Wv"], "wg": full["L0_attn_Wg"],
        "wc": full["L0_attn_Wc"],
        "bc": full["L0_attn_bc"].reshape(1, C),
        "gate_wt": full["_gate_wt"], "up_wt": full["_up_wt"],
        "down_wt": full["_down_wt"],
        "gate_st": full["_gate_st"], "up_st": full["_up_st"],
        "down_s": full["L0_ffn_down_s"].reshape(1, C),
        "tri": full["_tri"], "ublk": full["_ublk"], "ones32": full["_ones32"],
        "ident": full["_ident"], "consts": consts_arr, "dyn": dyn,
    }
    return d


def _prep_shared(inputs):
    full = dict(inputs)
    for k in list(full.keys()):
        full[k] = np.asarray(full[k])
    full["_gate_wt"] = np.ascontiguousarray(full["L0_ffn_gate_w"].T).astype(np.float32)
    full["_up_wt"] = np.ascontiguousarray(full["L0_ffn_up_w"].T).astype(np.float32)
    full["_down_wt"] = np.ascontiguousarray(full["L0_ffn_down_w"].T).astype(np.float32)
    full["_gate_st"] = np.ascontiguousarray(
        full["L0_ffn_gate_s"].reshape(NI, 128).T).astype(np.float32)
    full["_up_st"] = np.ascontiguousarray(
        full["L0_ffn_up_s"].reshape(NI, 128).T).astype(np.float32)
    full["_tri"] = np.triu(np.ones((128, 128), dtype=np.float32)).astype(_BF)
    full["_ublk"] = np.triu(np.ones((128, 128), dtype=np.float32))
    full["_ones32"] = np.ones((32, 128), dtype=np.float32)
    full["_ident"] = np.eye(128, dtype=np.float32)
    return full


def kernel(**inputs):
    full = _prep_shared(inputs)
    nc = build_program()
    in_maps = [_host_inputs(c, full) for c in range(8)]
    res = run_bass_kernel_spmd(nc, in_maps, core_ids=list(range(8)))
    out = np.zeros((B, T, C), dtype=np.float32)
    for core in range(8):
        b, half = core // 2, core % 2
        oo = res.results[core]["out_own"]
        for os in range(4):
            out[b, 1024 * os + 512 * half:1024 * os + 512 * half + 512] = \
                oo[os * 512:(os + 1) * 512]
    return out


# version 05
# speedup vs baseline: 1.0727x; 1.0255x over previous
import numpy as np
import ml_dtypes
from contextlib import ExitStack

import concourse.bass as bass
import concourse.mybir as mybir
from concourse.bass_utils import run_bass_kernel_spmd

F32 = mybir.dt.float32
BF16 = mybir.dt.bfloat16
I32 = mybir.dt.int32
AF = mybir.ActivationFunctionType
ALU = mybir.AluOpType
AX = mybir.AxisListType
ds = bass.ds

B, T, C = 4, 4096, 2048
H, D = 8, 128
QD, KD = 2048, 1024
INTER = 5632
NI = INTER // 128
GATE_OFFSET = 6.906768
OWN = 2048
NBLK = T // 128
VBLK = NBLK + 4
C_BIAS = -0.5 * float(np.log(128.0))
EPS_BIAS = float(np.log(1e-6))

CONSTS = {
    "c_bias": C_BIAS, "eps_bias": EPS_BIAS, "ln_eps": 1e-5, "one": 1.0,
    "go": GATE_OFFSET, "zero": 0.0, "inv_c": 1.0 / 2048.0, "neg1": -1.0,
}
CONST_COLS = list(CONSTS.keys())

J_KOFF, J_VOFF, J_CV, J_EPS = 0, 1, 2, 3
P1_SUB = 2
P1_GRAN = 6
P2_EVAC = "act"


class TL:
    def __init__(self, name):
        self.name, self.n, self.sem = name, 0, None


class Emitter:
    ENGS = ("tensor", "vector", "scalar", "gpsimd", "sync")

    def __init__(self):
        self.ops = {k: [] for k in self.ENGS}

    def wait(self, eng, tl, n):
        if n and n > 0:
            self.ops[eng].append(lambda e, s=tl, v=n: e.wait_ge(s.sem, v))

    def do(self, eng, f, inc=None, inc_k=1):
        if inc is None:
            self.ops[eng].append(f)
            return None
        inc.n += inc_k
        self.ops[eng].append(lambda e, f=f, s=inc, k=inc_k: f(e).then_inc(s.sem, k))
        return inc.n

    def replay(self, nc):
        with nc.Block() as blk:
            for name in self.ENGS:
                oplist = self.ops[name]
                if not oplist:
                    continue

                def body(e, oplist=oplist):
                    for f in oplist:
                        f(e)
                getattr(blk, name)(body)


_REGC = [0]
def _rn():
    _REGC[0] += 1
    return f"dynr{_REGC[0]}"


def build_program(debug=(), max_phase=5, detect_races=True):
    nc = bass.Bass("TRN2", target_bir_lowering=False,
                   detect_race_conditions=detect_races)
    dbg = set(debug)

    def scratch(name, shape, dt):
        return nc.dram_tensor(name, shape, dt,
                              kind="ExternalOutput" if name in dbg else "Internal")

    x_all = nc.dram_tensor("x_all", [T, C], F32, kind="ExternalInput")
    x_own = nc.dram_tensor("x_own", [OWN, C], F32, kind="ExternalInput")
    wq = nc.dram_tensor("wq", [C, QD], F32, kind="ExternalInput")
    wk = nc.dram_tensor("wk", [C, KD], F32, kind="ExternalInput")
    wv = nc.dram_tensor("wv", [C, KD], F32, kind="ExternalInput")
    wg = nc.dram_tensor("wg", [C, H], F32, kind="ExternalInput")
    wc = nc.dram_tensor("wc", [QD, C], F32, kind="ExternalInput")
    bc = nc.dram_tensor("bc", [1, C], F32, kind="ExternalInput")
    gate_wt = nc.dram_tensor("gate_wt", [C, INTER], F32, kind="ExternalInput")
    up_wt = nc.dram_tensor("up_wt", [C, INTER], F32, kind="ExternalInput")
    down_wt = nc.dram_tensor("down_wt", [INTER, C], F32, kind="ExternalInput")
    gate_st = nc.dram_tensor("gate_st", [128, NI], F32, kind="ExternalInput")
    up_st = nc.dram_tensor("up_st", [128, NI], F32, kind="ExternalInput")
    down_s = nc.dram_tensor("down_s", [1, C], F32, kind="ExternalInput")
    tri = nc.dram_tensor("tri", [128, 128], BF16, kind="ExternalInput")
    ublk = nc.dram_tensor("ublk", [128, 128], F32, kind="ExternalInput")
    ones32 = nc.dram_tensor("ones32", [32, 128], F32, kind="ExternalInput")
    ident = nc.dram_tensor("ident", [128, 128], F32, kind="ExternalInput")
    consts = nc.dram_tensor("consts", [128, len(CONST_COLS)], F32, kind="ExternalInput")
    dyn = nc.dram_tensor("dyn", [1, 96], I32, kind="ExternalInput")

    out_own = nc.dram_tensor("out_own", [OWN, C], F32, kind="ExternalOutput")

    xn_scr = scratch("xn_scr", [T, C], BF16)
    xn_own_scr = scratch("xn_own_scr", [OWN, C], BF16)
    q_scr = scratch("q_scr", [QD, OWN], BF16)
    k_scr = scratch("k_scr", [KD, T + 1024], BF16)
    v_scr = scratch("v_scr", [T + 1024, 1032], BF16)
    y_scr = scratch("y_scr", [OWN, QD], BF16)
    gsig_scr = scratch("gsig_scr", [OWN, C], F32)
    tmp_scr = scratch("tmp_scr", [32, 8], F32)
    logg_scr = scratch("logg_scr", [128, 256], F32)
    cg_scr = scratch("cg_scr", [128, 320], F32)
    eps_scr = scratch("eps_scr", [128, 256], F32)
    hdn_scr = scratch("hdn_scr", [NI, 128, 1024], BF16)
    sg_scr = scratch("sg_scr", [4, 128, 1024], BF16)
    dbg_scr = scratch("dbg_scr", [128, 1024], F32)
    ab_scr = scratch("ab_scr", [8, 128, 512], BF16)
    py_scr = scratch("py_scr", [128, 258], F32)
    qt_scr = scratch("qt_scr", [128, 512], BF16)

    with ExitStack() as outer:
        oe = outer.enter_context

        pe, act, dve, pool = TL("pe"), TL("act"), TL("dve"), TL("pool")
        qs, qg, qc = TL("qs"), TL("qg"), TL("qc")
        alltl = [pe, act, dve, pool, qs, qg, qc]
        for tl in alltl:
            tl.sem = oe(nc.semaphore(tl.name))

        consts_sb = oe(nc.sbuf_tensor("consts_sb", [128, len(CONST_COLS)], F32))
        dyn_sb = oe(nc.sbuf_tensor("dyn_sb", [1, 96], I32))
        tri_sb = oe(nc.sbuf_tensor("tri_sb", [128, 128], BF16))
        ublk_sb = oe(nc.sbuf_tensor("ublk_sb", [128, 128], F32))
        ones32_sb = oe(nc.sbuf_tensor("ones32_sb", [32, 128], F32))
        ident_sb = oe(nc.sbuf_tensor("ident_sb", [128, 128], F32))
        ones1f = oe(nc.sbuf_tensor("ones1f", [1, 128], F32))
        gpt = oe(nc.sbuf_tensor("gpt", [128, 256], F32))
        logg_pt = oe(nc.sbuf_tensor("logg_pt", [128, 256], F32))
        c_virt = oe(nc.sbuf_tensor("c_virt", [128, VBLK * 8], F32))
        eps_own = oe(nc.sbuf_tensor("eps_own", [128, 16 * 8], F32))
        bsum = oe(nc.sbuf_tensor("bsum", [32, 8], F32))

        def cc(name):
            i = CONST_COLS.index(name)
            return consts_sb[:, i:i + 1]

        for k, v in CONSTS.items():
            nc.const_aps.register(F32, float(v), cc(k))

        def dyn_gather(dst_ap, src_dram, j, width, max_val, rows=False):
            def f(e):
                with e.register(_rn()) as r:
                    e.reg_load(r, dyn_sb[0:1, j:j + 1])
                    off = e.snap(r, donate=True, min_val=0, max_val=max_val)
                    if rows:
                        s = src_dram[ds(off, width), :]
                    else:
                        s = src_dram[:, ds(off, width)]
                    return e.dma_start(dst_ap, s)
            return f

        def barrier(emm):
            snap = [(tl, tl.n) for tl in alltl]
            for eng in Emitter.ENGS:
                for tl, n in snap:
                    emm.wait(eng, tl, n)

        em = Emitter()
        for sb, dr in ((consts_sb, consts), (dyn_sb, dyn), (tri_sb, tri),
                       (ublk_sb, ublk), (ones32_sb, ones32), (ident_sb, ident)):
            em.do("sync", lambda e, sb=sb, dr=dr: e.dma_start(sb[:], dr[:, :]),
                  inc=qs, inc_k=16)
        em.do("vector", lambda e: e.memset(ones1f[:], 1.0), inc=dve)
        em.replay(nc)

        if max_phase < 1:
            return nc
        with ExitStack() as ph:
            pp = ph.enter_context
            em = Emitter()
            barrier(em)
            NR = 5
            NQ = 4
            x_b = [pp(nc.sbuf_tensor(f"a_x{i}", [128, C], F32)) for i in range(NR)]
            xq_b = [pp(nc.sbuf_tensor(f"a_xq{i}", [128, C], F32)) for i in range(NQ)]
            xn_b = [pp(nc.sbuf_tensor(f"a_xn{i}", [128, C], BF16)) for i in range(3)]
            st_b = [pp(nc.sbuf_tensor(f"a_st{i}", [128, 32], F32)) for i in range(NR)]

            tiles = [(x_all, xn_scr, i) for i in range(32)] + \
                    [(x_own, xn_own_scr, i) for i in range(16)]
            NT = len(tiles)
            x_free = [0] * NR
            x_free_a = [0] * NR
            st_free = [0] * NR
            xn_free = [0] * 3
            ld = {}; sqc = {}; s2c = {}; ngc = {}; vvc = {}; sdc = {}; xcc = {}
            xnc = {}
            for r in range(NT + 3):
                t = r
                if t < NT:
                    src_d, dst_d, i = tiles[t]
                    bi = t % NR
                    xb, stb = x_b[bi], st_b[bi]
                    em.wait("sync", dve, x_free[bi])
                    ld[t] = em.do("sync", lambda e, xb=xb, src_d=src_d, i=i:
                        e.dma_start(xb[:], src_d[i * 128:(i + 1) * 128, :]),
                        inc=qs, inc_k=16)
                    em.wait("vector", qs, ld[t])
                    em.wait("vector", act, st_free[bi])
                    for ch in range(4):
                        em.do("vector", lambda e, xb=xb, stb=stb, ch=ch:
                            e.bn_stats(stb[:, 8 + ch * 6:8 + (ch + 1) * 6],
                                       xb[:, ch * 512:(ch + 1) * 512]),
                            inc=dve if ch == 3 else None)
                    sqc[t] = dve.n
                t = r - 1
                if 0 <= t < NT:
                    bi = t % NR
                    stb = st_b[bi]
                    s2c[t] = em.do("vector", lambda e, stb=stb: e.bn_aggr(
                        stb[:, 0:2], stb[:, 8:32]), inc=dve)
                t = r - 2
                if 0 <= t < NT:
                    bi = t % NR
                    stb = st_b[bi]
                    em.wait("scalar", dve, s2c[t])
                    em.do("scalar", lambda e, stb=stb: e.activation(
                        stb[:, 5:6], stb[:, 1:2], AF.Sqrt,
                        bias=cc("ln_eps")), inc=act)
                    ngc[t] = em.do("scalar", lambda e, stb=stb: e.mul(
                        stb[:, 2:3], stb[:, 0:1], -1.0), inc=act)
                    sdc[t] = ngc[t]
                t = r - 3
                if 0 <= t < NT:
                    src_d, dst_d, i = tiles[t]
                    bi = t % NR
                    xb, stb, xqb = x_b[bi], st_b[bi], xq_b[t % NQ]
                    xnb = xn_b[t % 3]
                    em.wait("vector", act, sdc[t])
                    x_free_a[bi] = 0
                    em.do("vector", lambda e, stb=stb: e.reciprocal(
                        stb[:, 6:7], stb[:, 5:6]), inc=dve)
                    xcc[t] = em.do("vector", lambda e, xb=xb, xqb=xqb, stb=stb:
                        e.tensor_scalar(xqb[:], xb[:], stb[:, 2:3], None,
                                        ALU.add), inc=dve)
                    x_free[bi] = xcc[t]
                    em.wait("scalar", dve, xcc[t])
                    em.wait("scalar", qs, xn_free[t % 3])
                    xnc[t] = em.do("scalar", lambda e, xqb=xqb, xnb=xnb, stb=stb:
                        e.activation(xnb[:], xqb[:], AF.Copy, scale=stb[:, 6:7]),
                        inc=act)
                    st_free[bi] = xnc[t]
                    em.wait("sync", act, xnc[t])
                    xn_free[t % 3] = em.do("sync",
                        lambda e, xnb=xnb, dst_d=dst_d, i=i: e.dma_start(
                            dst_d[i * 128:(i + 1) * 128, :], xnb[:]),
                        inc=qs, inc_k=16)
            em.replay(nc)

        if max_phase < 2:
            return nc
        with ExitStack() as ph:
            pp = ph.enter_context
            em = Emitter()
            barrier(em)
            wq_sb = [pp(nc.sbuf_tensor(f"wq{k}", [128, QD], BF16)) for k in range(16)]
            wk_sb = [pp(nc.sbuf_tensor(f"wk{k}", [128, KD], BF16)) for k in range(16)]
            wv_sb = [pp(nc.sbuf_tensor(f"wv{k}", [128, KD], BF16)) for k in range(16)]
            wg_sb = [pp(nc.sbuf_tensor(f"wg{k}", [128, H], BF16)) for k in range(16)]
            for k in range(16):
                for sb, dr in ((wq_sb, wq), (wk_sb, wk), (wv_sb, wv), (wg_sb, wg)):
                    em.do("gpsimd", lambda e, sb=sb, dr=dr, k=k: e.dma_start(
                        sb[k][:], dr[k * 128:(k + 1) * 128, :]), inc=qg, inc_k=16)
            w_ld = qg.n

            xnf = [[pp(nc.sbuf_tensor(f"xnf{s}_{k}", [128, 512], BF16))
                    for k in range(16)] for s in range(2)]
            kv_st = [pp(nc.sbuf_tensor(f"kvst{i}", [128, 512], BF16)) for i in range(4)]
            v_st = [pp(nc.sbuf_tensor(f"vst{i}", [128, 1032], BF16)) for i in range(4)]
            g_st = [pp(nc.sbuf_tensor(f"gst{i}", [8, 512], F32)) for i in range(2)]
            zt = pp(nc.sbuf_tensor("zt", [128, 256], F32))
            azt = pp(nc.sbuf_tensor("azt", [128, 256], F32))
            zt2 = pp(nc.sbuf_tensor("zt2", [128, 256], F32))

            ps_a = [pp(nc.psum_tensor(f"ps_a{i}", [128, 512], F32)) for i in range(2)]
            ps_v = [pp(nc.psum_tensor(f"ps_v{i}", [128, 1024], F32)) for i in range(2)]
            ps_g = pp(nc.psum_tensor("ps_g", [8, 512], F32))
            ps_t = pp(nc.psum_tensor("ps_t", [128, 8], F32))

            em.do("vector", lambda e: e.memset(kv_st[0][:], 0.0), inc=dve)
            em.do("vector", lambda e: e.memset(v_st[0][:], 0.0), inc=dve)
            zf = dve.n
            em.wait("sync", dve, zf)
            for h in range(8):
                em.do("sync", lambda e, h=h: e.dma_start(
                    k_scr[h * 128:(h + 1) * 128, 0:512], kv_st[0][:]),
                    inc=qs, inc_k=16)
            for r4 in range(4):
                em.do("sync", lambda e, r4=r4: e.dma_start(
                    v_scr[r4 * 128:(r4 + 1) * 128, :], v_st[0][:]),
                    inc=qs, inc_k=16)
            zf_st = qs.n
            zf_st_g = qg.n
            for i in range(4):
                em.wait("vector", qs, zf_st)
                em.do("vector", lambda e, i=i: e.memset(
                    v_st[i][:, :].rearrange("p (h x) -> p h x", x=129)[:, :, 128:129],
                    1.0), inc=dve)
            ones_set = dve.n
            em.wait("tensor", qg, w_ld)

            xnf_ready = [0, 0]
            xnf_free = [0, 0]
            kv_free = [0, 0, 0, 0]
            v_free = [0, 0, 0, 0]
            em.wait("scalar", qs, zf_st)
            g_free = [0, 0]
            ps_a_free = [0, 0]
            ps_v_free = [0, 0]

            def load_xnf(sb, S, scr, w):
                for kc in range(16):
                    em.do("scalar", lambda e, sb=sb, kc=kc, S=S, scr=scr, w=w:
                          e.dma_start_transpose(
                              sb[kc][:, 0:w],
                              scr[S * w:(S + 1) * w, kc * 128:(kc + 1) * 128]),
                          inc=qc, inc_k=16)
                return qc.n

            def proj_block(xt, n_m, w_list, col, stage_mod, dst_fn):
                pa = proj_block.cnt % 2
                proj_block.cnt += 1
                em.wait("tensor", act, ps_a_free[pa])
                for k in range(16):
                    kwargs = dict(start=(k == 0), stop=(k == 15))
                    mmf = (lambda e, k=k, pa=pa, xt=xt, w=w_list, col=col:
                           e.matmul(ps_a[pa][:], w[k][:, col:col + 128], xt[k][:],
                                    **{"start": k == 0, "stop": k == 15}))
                    if k == 15:
                        em.do("tensor", mmf, inc=pe)
                    else:
                        em.do("tensor", mmf)
                mmdone = pe.n
                st = stage_mod % 4
                em.wait("scalar", qg, kv_free[st])
                em.wait("scalar", pe, mmdone)
                kvr = em.do("scalar", lambda e, pa=pa, st=st: e.activation(
                    kv_st[st][:], ps_a[pa][:], AF.Copy), inc=act)
                ps_a_free[pa] = act.n
                em.wait("gpsimd", act, kvr)
                kv_free[st] = em.do("gpsimd", dst_fn(kv_st[st]), inc=qg, inc_k=16)

            proj_block.cnt = 0

            for S in range(8):
                sb2 = S % 2
                em.wait("scalar", pe, xnf_free[sb2])
                xnf_ready[sb2] = load_xnf(xnf[sb2], S, xn_scr, 512)
                em.wait("tensor", qc, xnf_ready[sb2])
                xt = xnf[sb2]
                for h in range(8):
                    proj_block(xt, 128, wk_sb, h * 128, h,
                               lambda stg, h=h, S=S: (lambda e: e.dma_start(
                                   k_scr[h * 128:(h + 1) * 128,
                                         512 + S * 512:512 + (S + 1) * 512],
                                   stg[:])))
                for tc in range(4):
                    pv = tc % 2
                    em.wait("tensor", act, ps_v_free[pv])
                    for k in range(16):
                        for hn in range(2):
                            mmf = (lambda e, k=k, tc=tc, pv=pv, hn=hn, xt=xt:
                                   e.matmul(ps_v[pv][:, hn * 512:(hn + 1) * 512],
                                            xt[k][:, tc * 128:(tc + 1) * 128],
                                            wv_sb[k][:, hn * 512:(hn + 1) * 512],
                                            start=(k == 0), stop=(k == 15)))
                            if k == 15 and hn == 1:
                                em.do("tensor", mmf, inc=pe)
                            else:
                                em.do("tensor", mmf)
                    mmv = pe.n
                    vs = tc % 4
                    em.wait("scalar", dve, ones_set)
                    em.wait("scalar", qg, v_free[vs])
                    em.wait("scalar", pe, mmv)
                    vr = em.do("scalar", lambda e, pv=pv, vs=vs: e.activation(
                        v_st[vs][:, :].rearrange("p (h x) -> p h x", x=129)[:, :, 0:128],
                        ps_v[pv][:, :].rearrange("p (h x) -> p h x", x=128),
                        AF.Copy), inc=act)
                    ps_v_free[pv] = act.n
                    em.wait("gpsimd", act, vr)
                    v_free[vs] = em.do("gpsimd", lambda e, vs=vs, S=S, tc=tc:
                        e.dma_start(
                        v_scr[512 + S * 512 + tc * 128:512 + S * 512 + (tc + 1) * 128,
                              :], v_st[vs][:]), inc=qg, inc_k=16)
                gb = S % 2
                em.wait("tensor", dve, g_free[gb])
                for k in range(16):
                    mmf = (lambda e, k=k, xt=xt: e.matmul(
                        ps_g[:], wg_sb[k][:], xt[k][:],
                        start=(k == 0), stop=(k == 15)))
                    em.do("tensor", mmf, inc=pe if k == 15 else None)
                mmg = pe.n
                em.wait("vector", pe, mmg)
                gv = em.do("vector", lambda e, gb=gb: e.tensor_copy(
                    g_st[gb][:], ps_g[:]), inc=dve)
                for tc in range(4):
                    em.wait("tensor", dve, gv)
                    em.wait("tensor", dve, dve.n)
                    tr = em.do("tensor", lambda e, gb=gb, tc=tc: e.transpose(
                        ps_t[:], g_st[gb][:, tc * 128:(tc + 1) * 128],
                        ident_sb[0:8, 0:8]), inc=pe)
                    em.wait("vector", pe, tr)
                    em.do("vector", lambda e, S=S, tc=tc: e.tensor_copy(
                        gpt[:, (S * 4 + tc) * 8:(S * 4 + tc + 1) * 8], ps_t[:]),
                        inc=dve)
                g_free[gb] = dve.n
                xnf_free[sb2] = pe.n

            for So in range(4):
                sb2 = So % 2
                em.wait("scalar", pe, xnf_free[sb2])
                xnf_ready[sb2] = load_xnf(xnf[sb2], So, xn_own_scr, 512)
                em.wait("tensor", qc, xnf_ready[sb2])
                xt = xnf[sb2]
                for qd in range(16):
                    proj_block(xt, 128, wq_sb, qd * 128, qd,
                               lambda stg, qd=qd, So=So: (lambda e: e.dma_start(
                                   q_scr[qd * 128:(qd + 1) * 128,
                                         So * 512:(So + 1) * 512], stg[:])))
                xnf_free[sb2] = pe.n

            glast = dve.n
            em.wait("vector", dve, glast)
            zz = em.do("vector", lambda e: e.tensor_scalar(
                zt[:], gpt[:], GATE_OFFSET, None, ALU.add), inc=dve)
            em.wait("scalar", dve, zz)
            az = em.do("scalar", lambda e: e.activation(
                azt[:], zt[:], AF.Abs), inc=act)
            em.wait("vector", act, az)
            ne = em.do("vector", lambda e: e.tensor_scalar(
                zt2[:], azt[:], -1.0, None, ALU.mult), inc=dve)
            em.wait("scalar", dve, ne)
            en = em.do("scalar", lambda e: e.activation(
                azt[:], zt2[:], AF.Exp), inc=act)
            em.wait("vector", act, en)
            en1 = em.do("vector", lambda e: e.tensor_scalar(
                azt[:], azt[:], 1.0, None, ALU.add), inc=dve)
            em.wait("scalar", dve, en1)
            l1 = em.do("scalar", lambda e: e.activation(
                azt[:], azt[:], AF.Ln), inc=act)
            em.wait("vector", act, l1)
            em.do("vector", lambda e: e.scalar_tensor_tensor(
                gpt[:], zt[:], 0.0, azt[:], ALU.min, ALU.subtract), inc=dve)
            lgr = dve.n
            em.wait("tensor", dve, lgr)
            for blk in range(32):
                em.do("tensor", lambda e, blk=blk: e.matmul(
                    ps_t[:], ublk_sb[:], gpt[:, blk * 8:(blk + 1) * 8],
                    start=True, stop=True), inc=pe)
                em.wait("vector", pe, pe.n)
                em.do("vector", lambda e, blk=blk: e.tensor_copy(
                    logg_pt[:, blk * 8:(blk + 1) * 8], ps_t[:]), inc=dve)
                em.wait("tensor", dve, dve.n)
            cum_done = dve.n
            em.wait("sync", dve, cum_done)
            r127 = em.do("sync", lambda e: e.dma_start(
                tmp_scr[:, :].rearrange("(o b) h -> o (b h)", o=1),
                logg_pt[127:128, :]), inc=qs, inc_k=16)
            em.wait("sync", qs, r127)
            bs = em.do("sync", lambda e: e.dma_start(
                bsum[:, :], tmp_scr[:, :]), inc=qs, inc_k=16)
            em.wait("tensor", qs, bs)
            for blk in range(1, 32):
                em.do("tensor", lambda e, blk=blk: e.matmul(
                    ps_t[:], ones32_sb[0:blk, :], bsum[0:blk, :],
                    start=True, stop=True), inc=pe)
                em.wait("vector", pe, pe.n)
                em.do("vector", lambda e, blk=blk: e.tensor_tensor(
                    logg_pt[:, blk * 8:(blk + 1) * 8],
                    logg_pt[:, blk * 8:(blk + 1) * 8], ps_t[:], ALU.add), inc=dve)
                em.wait("tensor", dve, dve.n)
            logg_done = dve.n
            if "logg_scr" in dbg:
                em.wait("sync", dve, logg_done)
                em.do("sync", lambda e: e.dma_start(logg_scr[:, :], logg_pt[:]),
                      inc=qs, inc_k=16)
            c_all, eps_all = zt, azt
            em.wait("scalar", dve, logg_done)
            em.do("scalar", lambda e: e.activation(
                c_all[:], logg_pt[:], AF.Exp, scale=-0.5, bias=cc("c_bias")), inc=act)
            ea = em.do("scalar", lambda e: e.activation(
                eps_all[:], logg_pt[:], AF.Exp, scale=-1.0, bias=cc("eps_bias")),
                inc=act)
            zp = em.do("vector", lambda e: e.memset(c_virt[:, 0:64], 0.0), inc=dve)
            em.wait("sync", dve, zp)
            em.wait("sync", act, ea)
            em.do("sync", lambda e: e.dma_start(cg_scr[:, 0:32], c_virt[:, 0:32]),
                  inc=qs, inc_k=16)
            em.do("sync", lambda e: e.dma_start(cg_scr[:, 288:320], c_virt[:, 32:64]),
                  inc=qs, inc_k=16)
            em.do("sync", lambda e: e.dma_start(cg_scr[:, 32:288], c_all[:]),
                  inc=qs, inc_k=16)
            st_t = em.do("sync", lambda e: e.dma_start(eps_scr[:, :], eps_all[:]),
                         inc=qs, inc_k=16)
            em.wait("sync", qs, st_t)
            em.do("sync", dyn_gather(c_virt[:, :], cg_scr, J_CV, VBLK * 8, 32),
                  inc=qs, inc_k=16)
            for ob in range(4):
                em.do("sync", dyn_gather(
                    eps_own[:, ob * 32:(ob + 1) * 32],
                    eps_scr[:, 64 * ob:64 * ob + 64], J_EPS, 32, 32),
                    inc=qs, inc_k=16)
            em.replay(nc)

        if max_phase < 3:
            return nc
        with ExitStack() as ph:
            pp = ph.enter_context
            em = Emitter()
            barrier(em)
            xnfb = [pp(nc.sbuf_tensor(f"p1xn{k}", [128, 1024], BF16)) for k in range(16)]
            hdn = [pp(nc.sbuf_tensor(f"hdn{i}", [128, 1024], BF16)) for i in range(NI)]
            gw_b = [pp(nc.sbuf_tensor(f"gw{i}", [128, 2048], BF16)) for i in range(2)]
            uw_b = [pp(nc.sbuf_tensor(f"uw{i}", [128, 2048], BF16)) for i in range(2)]
            sg_b = [pp(nc.sbuf_tensor(f"sg{i}", [128, 1024], BF16)) for i in range(2)]
            uu_b = [pp(nc.sbuf_tensor(f"uu{i}", [128, 1024], BF16)) for i in range(2)]
            dw_ch = [pp(nc.sbuf_tensor(f"dwch{i}", [128, 11 * 512], BF16))
                     for i in range(2)]
            gst_sb = pp(nc.sbuf_tensor("gst_sb", [128, NI], F32))
            ust_sb = pp(nc.sbuf_tensor("ust_sb", [128, NI], F32))
            dsb_row = pp(nc.sbuf_tensor("dsb_row", [1, C], F32))
            ds_bc = pp(nc.sbuf_tensor("ds_bc", [128, C], F32))
            t1_b = [pp(nc.sbuf_tensor(f"t1_{i}", [128, 512], F32)) for i in range(2)]
            go_b = [pp(nc.sbuf_tensor(f"go_{i}", [128, 512], F32)) for i in range(2)]

            ps_gu = [pp(nc.psum_tensor(f"ps_gu{i}", [128, 1024], F32)) for i in range(4)]

            em.do("sync", lambda e: e.dma_start(gst_sb[:], gate_st[:, :]),
                  inc=qs, inc_k=16)
            em.do("sync", lambda e: e.dma_start(ust_sb[:], up_st[:, :]),
                  inc=qs, inc_k=16)
            em.do("sync", lambda e: e.dma_start(dsb_row[:], down_s[:, :]),
                  inc=qs, inc_k=16)
            sc_ld = qs.n
            em.wait("tensor", qs, sc_ld)
            for rnd in range(2):
                for hn in range(4):
                    em.do("tensor", lambda e, hn=hn, rnd=rnd: e.matmul(
                        ps_gu[0][:, hn * 256:(hn + 1) * 256], ones1f[:, :],
                        dsb_row[:, rnd * 1024 + hn * 256:rnd * 1024 + (hn + 1) * 256],
                        start=True, stop=True), inc=pe)
                em.wait("vector", pe, pe.n)
                em.do("vector", lambda e, rnd=rnd: e.tensor_copy(
                    ds_bc[:, rnd * 1024:(rnd + 1) * 1024], ps_gu[0][:]), inc=dve)
                em.wait("tensor", dve, dve.n)
            dsbc_done = dve.n

            for S in range(2 if P1_SUB >= 1 else 0):
                em.wait("scalar", pe, pe.n)
                for kc in range(16):
                    em.do("scalar", lambda e, kc=kc, S=S: e.dma_start_transpose(
                        xnfb[kc][:], xn_own_scr[S * 1024:(S + 1) * 1024,
                                                kc * 128:(kc + 1) * 128]),
                        inc=qc, inc_k=16)
                xn_ld = qc.n
                em.wait("tensor", qc, xn_ld)
                em.wait("tensor", dve, dsbc_done)
                em.wait("tensor", act, act.n)

                gw_free = [0] * 2
                hdn_done = 0
                for i in range(NI if P1_GRAN >= 2 else 0):
                    bw = i % 2
                    em.wait("gpsimd", pe, gw_free[bw])
                    em.do("gpsimd", lambda e, bw=bw, i=i: e.dma_start(
                        gw_b[bw][:, :].rearrange("p (k c) -> p k c", c=128),
                        gate_wt[:, i * 128:(i + 1) * 128].rearrange(
                            "(k p) c -> p k c", p=128)), inc=qg, inc_k=16)
                    em.do("gpsimd", lambda e, bw=bw, i=i: e.dma_start(
                        uw_b[bw][:, :].rearrange("p (k c) -> p k c", c=128),
                        up_wt[:, i * 128:(i + 1) * 128].rearrange(
                            "(k p) c -> p k c", p=128)), inc=qg, inc_k=16)
                    w_rdy = qg.n
                    pg, pu = ps_gu[(i % 2) * 2], ps_gu[(i % 2) * 2 + 1]
                    if P1_GRAN < 3:
                        continue
                    em.wait("tensor", qg, w_rdy)
                    em.wait("tensor", act, getattr(em, f"_p1sg{i%2}", 0) or 0)
                    em.wait("tensor", dve, getattr(em, f"_p1uu{i%2}", 0) or 0)
                    for k in range(16):
                        for hn in range(2):
                            mmf = (lambda e, k=k, hn=hn, bw=bw, pg=pg: e.matmul(
                                pg[:, hn * 512:(hn + 1) * 512],
                                gw_b[bw][:, k * 128:(k + 1) * 128],
                                xnfb[k][:, hn * 512:(hn + 1) * 512],
                                start=(k == 0), stop=(k == 15)))
                            em.do("tensor", mmf,
                                  inc=pe if (k == 15 and hn == 1) else None)
                    g_mm = pe.n
                    for k in range(16):
                        for hn in range(2):
                            mmf = (lambda e, k=k, hn=hn, bw=bw, pu=pu: e.matmul(
                                pu[:, hn * 512:(hn + 1) * 512],
                                uw_b[bw][:, k * 128:(k + 1) * 128],
                                xnfb[k][:, hn * 512:(hn + 1) * 512],
                                start=(k == 0), stop=(k == 15)))
                            em.do("tensor", mmf,
                                  inc=pe if (k == 15 and hn == 1) else None)
                    u_mm = pe.n
                    gw_free[bw] = pe.n
                    if P1_GRAN < 4:
                        continue
                    sgb, uub = sg_b[i % 2], uu_b[i % 2]
                    em.wait("scalar", pe, g_mm)
                    em.wait("scalar", dve, hdn_done)
                    sg_done = em.do("scalar", lambda e, sgb=sgb, pg=pg, i=i:
                        e.activation(sgb[:], pg[:], AF.Silu,
                                     scale=gst_sb[:, i:i + 1]), inc=act)
                    setattr(em, f"_p1sg{i%2}", act.n)
                    if "dbg_scr" in dbg and S == 0 and i == 0:
                        em.wait("vector", pe, g_mm)
                        pr = em.do("vector", lambda e, pg=pg: e.tensor_copy(
                            t1_b[0][:], pg[:, 0:512]), inc=dve)
                        em.wait("sync", dve, pr)
                        em.do("sync", lambda e: e.dma_start(
                            dbg_scr[:, 0:512], t1_b[0][:]), inc=qs, inc_k=16)
                        em.do("sync", lambda e: e.dma_start(
                            dbg_scr[0:128, 512:512 + NI].rearrange("p c -> p c"),
                            gst_sb[:, :]), inc=qs, inc_k=16)
                    if P1_GRAN < 5:
                        continue
                    em.wait("vector", pe, u_mm)
                    em.do("vector", lambda e, uub=uub, pu=pu, i=i: e.tensor_scalar(
                        uub[:], pu[:], ust_sb[:, i:i + 1], None, ALU.mult), inc=dve)
                    setattr(em, f"_p1uu{i%2}", dve.n)
                    if P1_GRAN < 6:
                        continue
                    em.wait("vector", act, sg_done)
                    hdn_done = em.do("vector", lambda e, i=i, sgb=sgb, uub=uub:
                        e.tensor_mul(hdn[i][:], sgb[:], uub[:]), inc=dve)
                    if "hdn_scr" in dbg and S == 0:
                        em.wait("sync", dve, hdn_done)
                        em.do("sync", lambda e, i=i: e.dma_start(
                            hdn_scr[i, :, :], hdn[i][:]), inc=qs, inc_k=16)
                        if i < 2:
                            em.do("sync", lambda e, i=i, sgb=sgb: e.dma_start(
                                sg_scr[i, :, :], sgb[:]), inc=qs, inc_k=16)
                            em.do("sync", lambda e, i=i, uub=uub: e.dma_start(
                                sg_scr[2 + i, :, :], uub[:]), inc=qs, inc_k=16)

                hdn_all = dve.n
                em.wait("tensor", dve, hdn_all)
                em.wait("tensor", dve, dsbc_done)
                go_free = [0] * 2
                t1_free = [0] * 2
                dw_free = [0] * 2
                ps_d = ps_gu
                for cs in range(4 if P1_SUB >= 2 else 0):
                    for th in range(2):
                        for ic in range(4):
                            bw2 = ic % 2
                            em.wait("gpsimd", pe, dw_free[bw2])
                            dwl = em.do("gpsimd", lambda e, cs=cs, ic=ic, bw2=bw2:
                                e.dma_start(
                                    dw_ch[bw2][:, :].rearrange(
                                        "p (i c) -> p i c", c=512),
                                    down_wt[ic * 1408:(ic + 1) * 1408,
                                            cs * 512:(cs + 1) * 512].rearrange(
                                        "(i p) c -> p i c", p=128)),
                                inc=qg, inc_k=16)
                            em.wait("tensor", qg, dwl)
                            for il in range(11):
                                i = ic * 11 + il
                                for tc in range(4):
                                    pd = ps_d[tc // 2][:, (tc % 2) * 512:
                                                       (tc % 2 + 1) * 512]
                                    mmf = (lambda e, i=i, il=il, tc=tc, th=th,
                                           pd=pd, bw2=bw2: e.matmul(
                                        pd, hdn[i][:, (th * 4 + tc) * 128:
                                                   (th * 4 + tc + 1) * 128],
                                        dw_ch[bw2][:, il * 512:(il + 1) * 512],
                                        start=(i == 0), stop=(i == NI - 1)))
                                    em.do("tensor", mmf,
                                          inc=pe if (il == 10 and tc == 3) else None)
                            dw_free[bw2] = pe.n
                        d_mm = pe.n
                        for tc in range(4):
                            pd = ps_d[tc // 2][:, (tc % 2) * 512:(tc % 2 + 1) * 512]
                            t1 = t1_b[tc % 2]
                            gob = go_b[tc % 2]
                            em.wait("vector", pe, d_mm)
                            em.wait("vector", act, t1_free[tc % 2])
                            d1 = em.do("vector", lambda e, pd=pd, t1=t1, cs=cs:
                                e.tensor_tensor(t1[:], pd,
                                                ds_bc[:, cs * 512:(cs + 1) * 512],
                                                ALU.mult), inc=dve)
                            em.wait("scalar", dve, d1)
                            em.wait("scalar", qs, go_free[tc % 2])
                            gs = em.do("scalar", lambda e, t1=t1, gob=gob:
                                e.activation(gob[:], t1[:], AF.Silu), inc=act)
                            t1_free[tc % 2] = act.n
                            em.wait("sync", act, gs)
                            go_free[tc % 2] = em.do("sync",
                                lambda e, gob=gob, S=S, th=th, tc=tc, cs=cs:
                                e.dma_start(
                                    gsig_scr[S * 1024 + th * 512 + tc * 128:
                                             S * 1024 + th * 512 + (tc + 1) * 128,
                                             cs * 512:(cs + 1) * 512], gob[:]),
                                inc=qs, inc_k=16)
                        em.wait("tensor", dve, dve.n)
            em.replay(nc)

        if max_phase < 4:
            return nc
        with ExitStack() as ph:
            pp = ph.enter_context
            em = Emitter()
            barrier(em)
            k_res = pp(nc.sbuf_tensor("kres", [128, 8 * VBLK * 128], BF16))
            v_res = pp(nc.sbuf_tensor("vres", [128, VBLK * 1032], BF16))
            q_b = [pp(nc.sbuf_tensor(f"qb{i}", [128, 512], BF16)) for i in range(2)]
            ab = [pp(nc.sbuf_tensor(f"ab{i}", [128, 512], BF16)) for i in range(4)]
            tmp_bf = pp(nc.sbuf_tensor("tmp_bf", [128, 512], BF16))
            dmy = pp(nc.sbuf_tensor("dmy", [1, 8], F32))
            dmy2 = pp(nc.sbuf_tensor("dmy2", [1, 8], F32))
            yn_b = [pp(nc.sbuf_tensor(f"ynb{i}", [128, 128], BF16)) for i in range(4)]
            ep_b = pp(nc.sbuf_tensor("ep_b", [128, 8], F32))

            py_dbg = pp(nc.sbuf_tensor("py_dbg", [128, 258], F32))
            ps_qk = [pp(nc.psum_tensor(f"ps_qk{i}", [128, 512], F32)) for i in range(2)]
            py = [pp(nc.psum_tensor(f"py{i}", [128, 129], F32)) for i in range(4)]

            def kload(e):
                with e.register(_rn()) as r:
                    e.reg_load(r, dyn_sb[0:1, J_KOFF:J_KOFF + 1])
                    off = e.snap(r, donate=True, min_val=0, max_val=512)
                    src = k_scr[:, ds(off, VBLK * 128)].rearrange(
                        "(h p) c -> p h c", p=128)
                    return e.dma_start(
                        k_res[:, :].rearrange("p (h c) -> p h c", c=VBLK * 128),
                        src)

            def vload(e):
                with e.register(_rn()) as r:
                    e.reg_load(r, dyn_sb[0:1, J_VOFF:J_VOFF + 1])
                    off = e.snap(r, donate=True, min_val=0, max_val=512)
                    src = v_scr[ds(off, VBLK * 128), :].rearrange(
                        "(b p) x -> p b x", p=128)
                    return e.dma_start(
                        v_res[:, :].rearrange("p (b x) -> p b x", x=1032),
                        src)

            em.do("sync", kload, inc=qs, inc_k=16)
            em.do("sync", vload, inc=qs, inc_k=16)
            kv_ld = qs.n
            em.wait("tensor", qs, kv_ld)

            q_ready = [0, 0]
            q_free = [0, 0]

            def q_load(idx, os, h):
                em.wait("sync", pe, q_free[idx])
                q_ready[idx] = em.do("sync", lambda e, os=os, h=h, idx=idx:
                    e.dma_start(q_b[idx][:],
                                q_scr[h * 128:(h + 1) * 128,
                                      os * 512:(os + 1) * 512]), inc=qs, inc_k=16)

            jobs = [(os, h) for os in range(4) for h in range(16)]
            q_load(0, *jobs[0])
            ep_act, ep_dve = 0, 0
            bank_evac = [None, None]
            abuf_mm2 = [0] * 4

            for ji, (os, h) in enumerate(jobs):
                kbmax = 8 * os + 8
                hv = h // 2
                qi = ji % 2
                if ji + 1 < len(jobs):
                    q_load(1 - qi, *jobs[ji + 1])
                em.wait("tensor", qs, q_ready[qi])
                evac_done = {}

                def emit_mm2(kb, kbmax=None, hv=None, evac_done=None):
                    j = kb - (kbmax - 4)
                    tcs = range(max(j, 0), 4)
                    etl, ecnt = evac_done[kb]
                    em.wait("tensor", etl, ecnt)
                    last_tc = max(tcs)
                    for tc in tcs:
                        stop = (kb == kbmax - 4 + tc)
                        pyt = py[tc][:, 0:129]
                        mmf = (lambda e, kb=kb, tc=tc, pyt=pyt, stop=stop, hv=hv:
                               e.matmul(pyt, ab[kb % 4][:, tc * 128:(tc + 1) * 128],
                                        v_res[:, kb * 1032 + hv * 129:
                                              kb * 1032 + hv * 129 + 129],
                                        start=(kb == 0), stop=stop))
                        em.do("tensor", mmf, inc=pe if tc == last_tc else None)
                    abuf_mm2[kb % 4] = pe.n

                mm1_cnts = {}
                for it in range(kbmax + 2):
                    if it < kbmax:
                        kb = it
                        pq = kb % 2
                        if bank_evac[pq] is not None:
                            etl, ecnt = bank_evac[pq]
                            em.wait("tensor", etl, ecnt)
                        if kb == 0:
                            em.wait("tensor", act, ep_act)
                            em.wait("tensor", dve, ep_dve)
                        j = kb - (kbmax - 4)
                        n0 = j * 128 if j > 0 else 0
                        kcol = hv * VBLK * 128 + kb * 128
                        mmf = (lambda e, kb=kb, pq=pq, n0=n0, kcol=kcol, qi=qi:
                               e.matmul(
                            ps_qk[pq][:, n0:512], k_res[:, kcol:kcol + 128],
                            q_b[qi][:, n0:512], start=True, stop=True))
                        em.do("tensor", mmf, inc=pe)
                        mm1_cnts[kb] = pe.n
                    if 0 <= it < kbmax:
                        kb = it
                        j = kb - (kbmax - 4)
                        pq = kb % 2
                        guard = mm1_cnts[kb]
                        own_guard = False
                        a_war = abuf_mm2[kb % 4]
                        ccol = c_virt[:, kb * 8 + hv:kb * 8 + hv + 1]
                        if kb % 2 == 0 or P2_EVAC == "act":
                            em.wait("scalar", pe, guard)
                            if a_war:
                                em.wait("scalar", pe, a_war)
                            if own_guard:
                                em.do("scalar", lambda e: e.mul(
                                    dmy2[:], dmy2[:], 1.0), inc=act)
                            ev = em.do("scalar", lambda e, kb=kb, pq=pq, ccol=ccol:
                                e.activation(ab[kb % 4][:], ps_qk[pq][:],
                                             AF.Square, scale=ccol), inc=act)
                            evac_done[kb] = (act, ev)
                            bank_evac[pq] = (act, ev)
                        else:
                            em.wait("vector", pe, guard)
                            if a_war:
                                em.wait("vector", pe, a_war)
                            if own_guard:
                                em.do("vector", lambda e: e.memset(dmy[:], 0.0),
                                      inc=dve)
                            ts_cnt = em.do("vector", lambda e, pq=pq, ccol=ccol:
                                e.tensor_scalar(
                                tmp_bf[:], ps_qk[pq][:], ccol, None, ALU.mult),
                                inc=dve)
                            bank_evac[pq] = (dve, ts_cnt)
                            em.do("vector", lambda e: e.memset(dmy[:], 0.0),
                                  inc=dve)
                            ev = em.do("vector", lambda e, kb=kb: e.tensor_mul(
                                ab[kb % 4][:], tmp_bf[:], tmp_bf[:]), inc=dve)
                            evac_done[kb] = (dve, ev)
                        if j >= 0:
                            etl, ecnt = evac_done[kb]
                            em.wait("vector", etl, ecnt)
                            if etl is dve:
                                em.do("vector", lambda e: e.memset(dmy[:], 0.0),
                                      inc=dve)
                            mk = em.do("vector", lambda e, kb=kb, j=j:
                                e.tensor_mul(
                                ab[kb % 4][:, j * 128:(j + 1) * 128],
                                ab[kb % 4][:, j * 128:(j + 1) * 128], tri_sb[:]),
                                inc=dve)
                            evac_done[kb] = (dve, mk)
                    if 0 <= it - 1 < kbmax:
                        emit_mm2(it - 1, kbmax=kbmax, hv=hv, evac_done=evac_done)
                q_free[qi] = pe.n

                all_mm2 = pe.n
                em.wait("vector", pe, all_mm2)
                em.wait("vector", act, ep_act)
                em.wait("scalar", pe, all_mm2)
                for tc in range(4):
                    oc = os * 4 + tc
                    pyt = py[tc]
                    base = 0
                    na = em.do("scalar", lambda e, pyt=pyt, base=base, tc=tc,
                               oc=oc, hv=hv:
                        e.add(ep_b[:, tc:tc + 1],
                              pyt[:, base + 128:base + 129],
                              eps_own[:, oc * 8 + hv:oc * 8 + hv + 1]), inc=act)
                    em.wait("vector", act, na)
                    rr = em.do("vector", lambda e, tc=tc: e.reciprocal(
                        ep_b[:, 4 + tc:5 + tc], ep_b[:, tc:tc + 1]), inc=dve)
                    em.wait("scalar", dve, rr)
                    em.wait("scalar", qs, getattr(em, f"_ynfree{tc}", 0) or 0)
                    yv = em.do("scalar", lambda e, pyt=pyt, base=base, tc=tc:
                        e.activation(yn_b[tc][:], pyt[:, base:base + 128], AF.Copy,
                                     scale=ep_b[:, 4 + tc:5 + tc]), inc=act)
                    em.wait("sync", act, yv)
                    st = em.do("sync", lambda e, tc=tc, os=os, h=h: e.dma_start(
                        y_scr[(os * 4 + tc) * 128:(os * 4 + tc + 1) * 128,
                              h * 128:(h + 1) * 128], yn_b[tc][:]), inc=qs, inc_k=16)
                    setattr(em, f"_ynfree{tc}", st)
                ep_act, ep_dve = act.n, dve.n
            em.replay(nc)

        if max_phase < 5:
            return nc
        with ExitStack() as ph:
            pp = ph.enter_context
            em = Emitter()
            barrier(em)
            wc_full = pp(nc.sbuf_tensor("wc_full", [128, 16 * 2048], BF16))
            yf = [[pp(nc.sbuf_tensor(f"yf{s}_{q}", [128, 512], BF16))
                   for q in range(16)] for s in range(2)]
            bc_row = pp(nc.sbuf_tensor("bc_row", [1, C], F32))
            bc_bc = pp(nc.sbuf_tensor("bc_bc", [128, C], F32))
            xg_b = [pp(nc.sbuf_tensor(f"xg{i}", [128, 512], F32)) for i in range(4)]
            gg_b = [pp(nc.sbuf_tensor(f"gg{i}", [128, 512], F32)) for i in range(4)]
            o_b = [pp(nc.sbuf_tensor(f"ob{i}", [128, 512], F32)) for i in range(4)]
            t2_b = [pp(nc.sbuf_tensor(f"t2_{i}", [128, 512], F32)) for i in range(2)]
            ps_s = [pp(nc.psum_tensor(f"ps_s{i}", [128, 512], F32)) for i in range(4)]
            ps_b = pp(nc.psum_tensor("ps_b", [128, 512], F32))

            em.do("gpsimd", lambda e: e.dma_start(
                wc_full[:, :].rearrange("p (q c) -> p q c", c=2048),
                wc[:, :].rearrange("(q p) c -> p q c", p=128)),
                inc=qg, inc_k=16)
            em.do("sync", lambda e: e.dma_start(bc_row[:], bc[:, :]), inc=qs, inc_k=16)
            bc_ld = qs.n
            wc_ld = qg.n
            em.wait("tensor", qs, bc_ld)
            for hn in range(4):
                em.do("tensor", lambda e, hn=hn: e.matmul(
                    ps_b[:], ones1f[:, :], bc_row[:, hn * 512:(hn + 1) * 512],
                    start=True, stop=True), inc=pe)
                em.wait("vector", pe, pe.n)
                em.do("vector", lambda e, hn=hn: e.tensor_copy(
                    bc_bc[:, hn * 512:(hn + 1) * 512], ps_b[:]), inc=dve)
                em.wait("tensor", dve, dve.n)
            em.wait("tensor", qg, wc_ld)

            xg_free = [0] * 4
            o_free = [0] * 4
            yf_free = [0, 0]
            y_lds = {}

            def load_y(os):
                ys = yf[os % 2]
                em.wait("scalar", pe, yf_free[os % 2])
                for qd in range(16):
                    em.do("scalar", lambda e, ys=ys, qd=qd, os=os:
                          e.dma_start_transpose(
                              ys[qd][:], y_scr[os * 512:(os + 1) * 512,
                                               qd * 128:(qd + 1) * 128]),
                          inc=qc, inc_k=16)
                y_lds[os] = qc.n

            load_y(0)
            for os in range(4):
                if os + 1 < 4:
                    load_y(os + 1)
                yfs = yf[os % 2]
                em.wait("tensor", qc, y_lds[os])
                for cs in range(4):
                    xg_ld = [0] * 4
                    for tc in range(4):
                        em.wait("sync", dve, xg_free[tc])
                        em.do("sync", lambda e, tc=tc, os=os, cs=cs: e.dma_start(
                            xg_b[tc][:], x_own[os * 512 + tc * 128:
                                               os * 512 + (tc + 1) * 128,
                                               cs * 512:(cs + 1) * 512]),
                            inc=qs, inc_k=16)
                        xg_ld[tc] = em.do("sync",
                            lambda e, tc=tc, os=os, cs=cs: e.dma_start(
                                gg_b[tc][:], gsig_scr[os * 512 + tc * 128:
                                                      os * 512 + (tc + 1) * 128,
                                                      cs * 512:(cs + 1) * 512]),
                            inc=qs, inc_k=16)
                    for qd in range(16):
                        for tc in range(4):
                            mmf = (lambda e, qd=qd, tc=tc, cs=cs, yfs=yfs:
                                   e.matmul(
                                ps_s[tc][:], yfs[qd][:, tc * 128:(tc + 1) * 128],
                                wc_full[:, qd * 2048 + cs * 512:
                                        qd * 2048 + (cs + 1) * 512],
                                start=(qd == 0), stop=(qd == 15)))
                            em.do("tensor", mmf,
                                  inc=pe if (qd == 15 and tc == 3) else None)
                    s_mm = pe.n
                    if cs == 3:
                        yf_free[os % 2] = pe.n
                    for tc in range(4):
                        t2 = t2_b[tc % 2]
                        em.wait("vector", pe, s_mm)
                        em.wait("vector", qs, xg_ld[tc])
                        em.do("vector", lambda e, tc=tc, t2=t2, cs=cs:
                            e.tensor_tensor(t2[:], ps_s[tc][:],
                                            bc_bc[:, cs * 512:(cs + 1) * 512],
                                            ALU.add), inc=dve)
                        em.do("vector", lambda e, tc=tc, t2=t2: e.tensor_mul(
                            t2[:], t2[:], gg_b[tc][:]), inc=dve)
                        em.wait("vector", qs, o_free[tc])
                        ov = em.do("vector", lambda e, tc=tc, t2=t2: e.tensor_add(
                            o_b[tc][:], t2[:], xg_b[tc][:]), inc=dve)
                        xg_free[tc] = dve.n
                        em.wait("sync", dve, ov)
                        o_free[tc] = em.do("sync",
                            lambda e, tc=tc, os=os, cs=cs: e.dma_start(
                                out_own[os * 512 + tc * 128:
                                        os * 512 + (tc + 1) * 128,
                                        cs * 512:(cs + 1) * 512], o_b[tc][:]),
                            inc=qs, inc_k=16)
                    em.wait("tensor", dve, dve.n)
            em.wait("sync", qs, qs.n)
            em.replay(nc)

    return nc


_BF = ml_dtypes.bfloat16


def _host_inputs(core, full):
    b, half = core // 2, core % 2
    x = full["L0_attn_hidden_states"]
    xb = np.ascontiguousarray(x[b]).astype(np.float32, copy=False)
    own = np.concatenate([xb[1024 * os + 512 * half:1024 * os + 512 * half + 512]
                          for os in range(4)], axis=0)
    shiftb = 4 * (1 - half)
    dyn = np.zeros((1, 96), dtype=np.int32)
    dyn[0, J_KOFF] = 512 - shiftb * 128
    dyn[0, J_VOFF] = 512 - shiftb * 128
    dyn[0, J_CV] = 32 - shiftb * 8
    dyn[0, J_EPS] = 32 * half
    consts_arr = np.zeros((128, len(CONST_COLS)), dtype=np.float32)
    for i, k in enumerate(CONST_COLS):
        consts_arr[:, i] = CONSTS[k]
    d = {
        "x_all": xb,
        "x_own": np.ascontiguousarray(own),
        "wq": full["L0_attn_Wq"], "wk": full["L0_attn_Wk"],
        "wv": full["L0_attn_Wv"], "wg": full["L0_attn_Wg"],
        "wc": full["L0_attn_Wc"],
        "bc": full["L0_attn_bc"].reshape(1, C),
        "gate_wt": full["_gate_wt"], "up_wt": full["_up_wt"],
        "down_wt": full["_down_wt"],
        "gate_st": full["_gate_st"], "up_st": full["_up_st"],
        "down_s": full["L0_ffn_down_s"].reshape(1, C),
        "tri": full["_tri"], "ublk": full["_ublk"], "ones32": full["_ones32"],
        "ident": full["_ident"], "consts": consts_arr, "dyn": dyn,
    }
    return d


def _prep_shared(inputs):
    full = dict(inputs)
    for k in list(full.keys()):
        full[k] = np.asarray(full[k])
    full["_gate_wt"] = np.ascontiguousarray(full["L0_ffn_gate_w"].T).astype(np.float32)
    full["_up_wt"] = np.ascontiguousarray(full["L0_ffn_up_w"].T).astype(np.float32)
    full["_down_wt"] = np.ascontiguousarray(full["L0_ffn_down_w"].T).astype(np.float32)
    full["_gate_st"] = np.ascontiguousarray(
        full["L0_ffn_gate_s"].reshape(NI, 128).T).astype(np.float32)
    full["_up_st"] = np.ascontiguousarray(
        full["L0_ffn_up_s"].reshape(NI, 128).T).astype(np.float32)
    full["_tri"] = np.triu(np.ones((128, 128), dtype=np.float32)).astype(_BF)
    full["_ublk"] = np.triu(np.ones((128, 128), dtype=np.float32))
    full["_ones32"] = np.ones((32, 128), dtype=np.float32)
    full["_ident"] = np.eye(128, dtype=np.float32)
    return full


def kernel(**inputs):
    full = _prep_shared(inputs)
    nc = build_program()
    in_maps = [_host_inputs(c, full) for c in range(8)]
    res = run_bass_kernel_spmd(nc, in_maps, core_ids=list(range(8)))
    out = np.zeros((B, T, C), dtype=np.float32)
    for core in range(8):
        b, half = core // 2, core % 2
        oo = res.results[core]["out_own"]
        for os in range(4):
            out[b, 1024 * os + 512 * half:1024 * os + 512 * half + 512] = \
                oo[os * 512:(os + 1) * 512]
    return out
